# version 04
# baseline (speedup 1.0000x reference)
import numpy as np
from contextlib import ExitStack

import concourse.bass as bass
import concourse.mybir as mybir
from concourse.bass_utils import run_bass_kernel_spmd

F32 = mybir.dt.float32
BF16 = mybir.dt.bfloat16
I8 = mybir.dt.int8
AF = mybir.ActivationFunctionType
ALU = mybir.AluOpType

D = 2048
T = 2048
SEQ = 4096
BATCH = 4
DIN = 4096
DXB = 1024
NH = 64
INTER = 8192
DPROJ = 10304
OT_X, OT_B, OT_C, OT_DT = 32, 40, 48, 80
NT = 16
KD = 16
EPS = 1e-5
GELU_K = 1.5957691216057308

_STOP_AFTER = None
_TAPS = ()
_DEV_CORES = None
_last_results = None

ENGS = ("pe", "act", "dve", "pool", "sp")
_EATTR = {"pe": "tensor", "act": "scalar", "dve": "vector", "pool": "gpsimd", "sp": "sync"}

_ORDER = ["prelude", "P0", "P1", "spill", "P0p", "P15a", "P15b",
          "P3a", "P2", "P23b", "P4", "P5a", "P5b"]


def _on(name):
    if _STOP_AFTER is None:
        return True
    return _ORDER.index(name) <= _ORDER.index(_STOP_AFTER)


def dap(t, offset, dims):
    return bass.AP(t, offset, [list(d) for d in dims])


class _Op:
    __slots__ = ("eng", "emit", "deps", "inck", "inc", "idx", "tgt", "isdma", "skey")

    def __init__(self, eng, emit, isdma, skey):
        self.eng = eng
        self.emit = emit
        self.deps = []
        self.inck = 16 if isdma else 1
        self.inc = False
        self.idx = -1
        self.tgt = None
        self.isdma = isdma
        self.skey = skey


class Sched:

    def __init__(self, nc, st):
        self.nc = nc
        self.st = st
        self.sem = {e: st.enter_context(nc.semaphore("sem_" + e)) for e in ENGS}
        self.cnt = {k: 0 for k in ENGS}
        self.blk_id = 0
        self.ops = None
        self.writer = {}
        self.readers = {}
        self.nsem = 0

    def begin(self, name=None):
        assert self.ops is None
        self.scope = name or f"b{self.blk_id}"
        self.ops = {e: [] for e in ENGS}

    def add(self, eng, emit, reads=(), writes=(), dma=False, deps=(), stream=None):
        skey = eng
        if dma:
            if stream is None:
                for r in writes:
                    stream = r
                    break
            if stream is None:
                for r in reads:
                    stream = r
                    break
            skey = (eng, stream)
            if skey not in self.sem:
                self.sem[skey] = self.st.enter_context(
                    self.nc.semaphore(f"dsem{self.nsem}"))
                self.nsem += 1
                self.cnt[skey] = 0
        op = _Op(eng, emit, dma, skey)
        dd = set()
        for r in reads:
            w = self.writer.get(r)
            if w is not None and w[0] == self.blk_id and w[1] is not op:
                dd.add(w[1])
            self.readers.setdefault(r, []).append((self.blk_id, op))
        for r in writes:
            w = self.writer.get(r)
            if w is not None and w[0] == self.blk_id:
                dd.add(w[1])
            for (b, rd) in self.readers.get(r, ()):
                if b == self.blk_id and rd is not op:
                    dd.add(rd)
            self.writer[r] = (self.blk_id, op)
            self.readers[r] = []
        for d in deps:
            if d is not None:
                dd.add(d)
        op.deps = list(dd)
        self.ops[eng].append(op)
        return op

    def end(self):
        ops = self.ops
        for e in ENGS:
            for op in ops[e]:
                if op.isdma:
                    op.inc = True
                for d in op.deps:
                    d.inc = True
        cnt = dict(self.cnt)
        for e in ENGS:
            for op in ops[e]:
                op.idx = len(ops[e])
        for e in ENGS:
            for i, op in enumerate(ops[e]):
                op.idx = i
                if op.inc:
                    cnt[op.skey] = cnt.get(op.skey, 0) + op.inck
                op.tgt = cnt.get(op.skey, 0)
        for e in ENGS:
            for op in ops[e]:
                for d in op.deps:
                    if d.eng == e:
                        assert d.idx < op.idx, "same-engine dep must precede"
        self.cnt.update(cnt)
        nc, sem = self.nc, self.sem
        with nc.named_scope(self.scope), nc.Block() as blk:
            for e in ENGS:
                if not ops[e]:
                    continue

                def body(eng_obj, e=e):
                    hw = {}
                    for op in ops[e]:
                        want = {}
                        for d in op.deps:
                            k = d.skey
                            want[k] = max(want.get(k, 0), d.tgt)
                        for k, v in want.items():
                            if v > hw.get(k, 0):
                                eng_obj.wait_ge(sem[k], v)
                                hw[k] = v
                        ins = op.emit(eng_obj)
                        if op.inc:
                            ins.then_inc(sem[op.skey], op.inck)
                    drains = {}
                    for o in ops[e]:
                        if o.isdma and o.inc:
                            drains[o.skey] = cnt[o.skey]
                    for k, v in drains.items():
                        if v > hw.get(k, 0):
                            eng_obj.wait_ge(sem[k], v)

                getattr(blk, _EATTR[e])(body)
        self.ops = None
        self.blk_id += 1


def build_nc():
    nc = bass.Bass("TRN2", target_bir_lowering=False)

    ein = lambda n, s, d: nc.dram_tensor(n, s, d, kind="ExternalInput")
    xb = ein("xb", [SEQ, D], F32)
    winT = ein("winT", [D, DPROJ], F32)
    lnw_r = ein("lnw_r", [D], F32)
    lnb_r = ein("lnb_r", [D], F32)
    dtb_c = ein("dtb_c", [NH, 1], F32)
    alog_c = ein("alog_c", [NH, 1], F32)
    dp_r = ein("dp_r", [1, NH], F32)
    nw_cols = ein("nw_cols", [128, 32], F32)
    woutT = ein("woutT", [DIN, D], F32)
    fc1T = ein("fc1T", [D, INTER], F32)
    fc1b_c = ein("fc1b_c", [128, 64], F32)
    fc2T = ein("fc2T", [INTER, D], F32)
    fc2b_r = ein("fc2b_r", [D], F32)
    wg_i = ein("wg_i", [D, INTER], I8)
    wu_i = ein("wu_i", [D, INTER], I8)
    wd_i = ein("wd_i", [INTER, D], I8)
    sg_c = ein("sg_c", [128, 64], F32)
    su_c = ein("su_c", [128, 64], F32)
    sd_r = ein("sd_r", [D], F32)
    ident_d = ein("ident_d", [128, 128], F32)
    tri_d = ein("tri_d", [128, 128], F32)
    trineg_d = ein("trineg_d", [128, 128], F32)
    ones_d = ein("ones_d", [1, 128], F32)
    rm_c = ein("rm_c", [NH, 1], F32)

    out = nc.dram_tensor("out", [T, D], F32, kind="ExternalOutput")

    d_res = nc.dram_tensor("d_res", [T, D], F32)
    d_h = nc.dram_tensor("d_h", [128, KD * T], BF16)
    d_zsil = nc.dram_tensor("d_zsil", [DIN, T], BF16)
    d_C = nc.dram_tensor("d_C", [DIN, T], BF16)
    d_xf = nc.dram_tensor("d_xf", [DXB, T], BF16)
    d_Bf = nc.dram_tensor("d_Bf", [DXB, T], BF16)
    d_xpf = nc.dram_tensor("d_xpf", [DXB, T], BF16)
    d_Bpf = nc.dram_tensor("d_Bpf", [DXB, T], BF16)
    d_expca = nc.dram_tensor("d_expca", [NH, T], BF16)
    d_yf = nc.dram_tensor("d_yf", [DIN, T], BF16)
    d_u = nc.dram_tensor("d_u", [INTER, T], BF16)
    d_fuse = nc.dram_tensor("d_fuse", [INTER, T], BF16)

    scrmap = {"d_res": (d_res, [T, D]), "d_zsil": (d_zsil, [DIN, T]),
              "d_C": (d_C, [DIN, T]), "d_xf": (d_xf, [DXB, T]),
              "d_Bf": (d_Bf, [DXB, T]), "d_xpf": (d_xpf, [DXB, T]),
              "d_Bpf": (d_Bpf, [DXB, T]), "d_yf": (d_yf, [DIN, T]),
              "d_u": (d_u, [INTER, T]), "d_fuse": (d_fuse, [INTER, T]),
              "d_h": (d_h, [128, KD * T]), "d_expca": (d_expca, [NH, T])}
    taps = []
    for name in _TAPS:
        th, shp = scrmap[name]
        dt = F32 if name == "d_res" else BF16
        taps.append((th, nc.dram_tensor("tap_" + name, shp, dt, kind="ExternalOutput"), shp))

    stG = ExitStack()
    with stG:
        sbG = lambda n, s, d: stG.enter_context(nc.sbuf_tensor(n, s, d))
        ps = lambda n, s, d: stG.enter_context(nc.psum_tensor(n, s, d))

        g_cols = sbG("g_cols", [128, NT], F32)
        ident_f = sbG("ident_f", [128, 128], F32)
        ident_b = sbG("ident_b", [128, 128], BF16)
        tri_f = sbG("tri_f", [128, 128], F32)
        trineg4_b = sbG("trineg4_b", [128, 512], BF16)
        ones_f = sbG("ones_f", [1, 128], F32)
        ones_b = sbG("ones_b", [1, 128], BF16)
        onescol_b = sbG("onescol_b", [128, 1], BF16)
        dtb_sb = sbG("dtb_sb", [NH, 1], F32)
        nega_sb = sbG("nega_sb", [NH, 1], F32)
        dp_sb = sbG("dp_sb", [1, NH], F32)
        dpcols = sbG("dpcols", [NH, NH], F32)
        rm_sb = sbG("rm_sb", [NH, 1], F32)
        cst_eps = sbG("cst_eps", [128, 1], F32)
        cst_one = sbG("cst_one", [128, 1], F32)

        pA = ps("pA", [128, 512], F32)
        pB = ps("pB", [128, 512], F32)
        pC = ps("pC", [128, 512], F32)
        pD = ps("pD", [64, 512], F32)
        pG = ps("pG", [128, 512], F32)
        pT2 = [ps("pT0", [128, 1024], BF16), ps("pT1", [128, 1024], BF16)]
        pF = ps("pF", [128, 512], F32)

        sc = Sched(nc, stG)

        sc.begin()
        sc.add("sp", lambda e: e.dma_start(out=ident_f[:], in_=ident_d.ap()), writes=["ident_f"], dma=True)
        sc.add("sp", lambda e: e.dma_start(out=tri_f[:], in_=tri_d.ap()), writes=["tri_f"], dma=True)
        sc.add("sp", lambda e: e.dma_start(out=ones_f[:], in_=ones_d.ap()), writes=["ones_f"], dma=True)
        sc.add("sp", lambda e: e.dma_start(out=dtb_sb[:], in_=dtb_c.ap()), writes=["dtb"], dma=True)
        sc.add("sp", lambda e: e.dma_start(out=rm_sb[:], in_=rm_c.ap()), writes=["rm"], dma=True)
        sc.add("sp", lambda e: e.dma_start(out=dp_sb[:], in_=dp_r.ap()), writes=["dp"], dma=True)
        sc.add("sp", lambda e: e.dma_start(out=nega_sb[:], in_=alog_c.ap()), writes=["nega"], dma=True)
        sc.add("pool", lambda e: e.dma_start(out=ident_b[:], in_=ident_d.ap()), writes=["ident_b"], dma=True)
        sc.add("pool", lambda e: e.dma_start(out=ones_b[:], in_=ones_d.ap()), writes=["ones_b"], dma=True)
        sc.add("pool", lambda e: e.dma_start(out=trineg4_b[:], in_=dap(trineg_d, 0, [[128, 128], [0, 4], [1, 128]])), writes=["trineg"], dma=True)
        for _q in range(4):
            sc.add("pool", (lambda e, _q=_q: e.dma_start(
                out=d_res[_q * 512:(_q + 1) * 512, :],
                in_=dap(fc2b_r, 0, [[0, 512], [1, D]]))), writes=["d_res"], dma=True)
        sc.add("dve", lambda e: e.memset(onescol_b[:], 1.0), writes=["onescol"])
        sc.add("dve", lambda e: e.memset(cst_eps[:], EPS), writes=["cst_eps"])
        sc.add("dve", lambda e: e.memset(cst_one[:], 1.0), writes=["cst_one"])
        nc.const_aps.register(F32, EPS, cst_eps[:])
        nc.const_aps.register(F32, 1.0, cst_one[:])
        sc.add("act", lambda e: e.activation(nega_sb[:], nega_sb[:], AF.Exp), reads=["nega"], writes=["nega"])
        sc.add("pe", lambda e: e.matmul(pG[0:NH, 0:NH], ones_f[0:1, 0:NH], dp_sb[:], start=True, stop=True),
               reads=["ones_f", "dp"], writes=["pG"])
        sc.add("dve", lambda e: e.tensor_copy(dpcols[:], pG[0:NH, 0:NH]), reads=["pG"], writes=["dpcols"])
        sc.end()

        def emit_hpass(tag, hbuf, src_dram, src_base, *, affine, res_add):
            with ExitStack() as stP:
                sb = lambda n, s, d: stP.enter_context(nc.sbuf_tensor(n, s, d))
                xt = [sb(f"{tag}_xt{i}", [128, D], F32) for i in range(2)]
                xm = sb(f"{tag}_xm", [128, D], F32)
                sq = sb(f"{tag}_sq", [128, D], F32)
                h0 = sb(f"{tag}_h0", [128, D], F32)
                rt = sb(f"{tag}_rt", [128, D], F32)
                hbf = [sb(f"{tag}_hbf{i}", [128, D], BF16) for i in range(2)]
                cols = sb(f"{tag}_cols", [128, 8], F32)
                junk = sb(f"{tag}_junk", [128, 1], F32)
                if affine:
                    lnwb = sb(f"{tag}_lnw", [128, D], F32)
                    lnbb = sb(f"{tag}_lnb", [128, D], F32)

                sc.begin(tag)
                if affine:
                    sc.add("sp", lambda e: e.dma_start(out=lnwb[:], in_=dap(lnw_r, 0, [[0, 128], [1, D]])),
                           writes=["lnwb"], dma=True)
                    sc.add("sp", lambda e: e.dma_start(out=lnbb[:], in_=dap(lnb_r, 0, [[0, 128], [1, D]])),
                           writes=["lnbb"], dma=True)
                for i in range(NT):
                    xti = xt[i % 2]
                    Rx = f"xt{i%2}"
                    sc.add("sp", (lambda e, i=i, xti=xti: e.dma_start(
                        out=xti[:], in_=src_dram[src_base + i * 128: src_base + (i + 1) * 128, :])),
                        writes=[Rx], dma=True)
                    sc.add("act", (lambda e, xti=xti: e.activation(sq[:], xti[:], AF.Copy, accum_out=cols[:, 0:1])),
                           reads=[Rx], writes=["sq", "c0"])
                    sc.add("dve", lambda e: e.tensor_scalar_mul(cols[:, 0:1], cols[:, 0:1], 1.0 / D),
                           reads=["c0"], writes=["c0"])
                    sc.add("dve", (lambda e, xti=xti: e.tensor_scalar(xm[:], xti[:], cols[:, 0:1], None, op0=ALU.subtract)),
                           reads=[Rx, "c0"], writes=["xm"])
                    sc.add("act", lambda e: e.activation(sq[:], xm[:], AF.Square, accum_out=cols[:, 1:2]),
                           reads=["xm"], writes=["sq", "c1"])
                    sc.add("act", lambda e: e.activation(cols[:, 2:3], cols[:, 1:2], AF.Ln, bias=EPS, scale=1.0 / D),
                           reads=["c1"], writes=["c2"])
                    sc.add("act", lambda e: e.activation(cols[:, 3:4], cols[:, 2:3], AF.Exp, scale=-0.5),
                           reads=["c2"], writes=["c3"])
                    sc.add("dve", lambda e: e.tensor_scalar(h0[:], xm[:], cols[:, 3:4], None, op0=ALU.mult),
                           reads=["xm", "c3"], writes=["h0"])
                    if res_add:
                        sc.add("dve", (lambda e, xti=xti: e.tensor_add(rt[:], xti[:], h0[:])),
                               reads=[Rx, "h0"], writes=["rt"])
                        sc.add("pool", (lambda e, i=i: e.dma_start(
                            out=d_res[i * 128:(i + 1) * 128, :], in_=rt[:], accum_op=ALU.add)),
                            reads=["rt"], dma=True)
                    if affine:
                        sc.add("act", lambda e: e.activation(sq[:], h0[:], AF.Copy, accum_out=cols[:, 4:5]),
                               reads=["h0"], writes=["sq", "c4"])
                        sc.add("dve", lambda e: e.tensor_scalar_mul(cols[:, 4:5], cols[:, 4:5], 1.0 / D),
                               reads=["c4"], writes=["c4"])
                        sc.add("dve", lambda e: e.tensor_scalar(xm[:], h0[:], cols[:, 4:5], None, op0=ALU.subtract),
                               reads=["h0", "c4"], writes=["xm"])
                        sc.add("act", lambda e: e.activation(sq[:], xm[:], AF.Square, accum_out=cols[:, 5:6]),
                               reads=["xm"], writes=["sq", "c5"])
                        sc.add("act", lambda e: e.activation(cols[:, 6:7], cols[:, 5:6], AF.Ln, bias=EPS, scale=1.0 / D),
                               reads=["c5"], writes=["c6"])
                        sc.add("act", lambda e: e.activation(cols[:, 7:8], cols[:, 6:7], AF.Exp, scale=-0.5),
                               reads=["c6"], writes=["c7"])
                        sc.add("dve", lambda e: e.tensor_scalar(h0[:], xm[:], cols[:, 7:8], None, op0=ALU.mult),
                               reads=["xm", "c7"], writes=["h0"])
                        sc.add("dve", lambda e: e.tensor_mul(h0[:], h0[:], lnwb[:]),
                               reads=["h0", "lnwb"], writes=["h0"])
                        hb = hbf[i % 2]
                        sc.add("dve", (lambda e, hb=hb: e.tensor_add(hb[:], h0[:], lnbb[:])),
                               reads=["h0", "lnbb"], writes=[f"hbf{i%2}"])
                    else:
                        hb = hbf[i % 2]
                        sc.add("dve", (lambda e, hb=hb: e.tensor_scalar(hb[:], xm[:], cols[:, 3:4], None, op0=ALU.mult)),
                               reads=["xm", "c3"], writes=[f"hbf{i%2}"])
                    for k in range(KD):
                        hb = hbf[i % 2]
                        slot = k % 2
                        sc.add("pe", (lambda e, hb=hb, k=k, slot=slot: e.transpose(
                            pT2[slot][:, 0:128], hb[:, k * 128:(k + 1) * 128], ident_b[:])),
                            reads=[f"hbf{i%2}", "ident_b"], writes=[f"pT{slot}"])
                        sc.add("act", (lambda e, i=i, k=k, slot=slot: e.copy(
                            hbuf[:, k * T + i * 128: k * T + (i + 1) * 128],
                            pT2[slot][:, 0:128])),
                            reads=[f"pT{slot}"], writes=["hbuf"])
                sc.end()

        def emit_inproj(tag, hbuf, otiles, dsts, stf):
            with ExitStack() as stP:
                sb = lambda n, s, d: stP.enter_context(nc.sbuf_tensor(n, s, d))
                wb = [sb(f"{tag}_wb{i}", [128, KD * 128], BF16) for i in range(2)]
                stg = [sb(f"{tag}_st{i}", [128, 512], BF16) for i in range(4)]
                sc.begin(tag)
                for j, ot in enumerate(otiles):
                    wbj = wb[j % 2]
                    Rw = f"wb{j%2}"
                    wcols = 64 if ot == OT_DT else 128
                    for half in range(2):
                        sc.add("pool", (lambda e, wbj=wbj, ot=ot, half=half, wcols=wcols: e.dma_start(
                            out=wbj[:, half * 1024:(half + 1) * 1024].rearrange(
                                "p (k o) -> p k o", k=8)[:, :, 0:wcols],
                            in_=dap(winT, ot * 128 + half * 8 * 128 * DPROJ,
                                    [[DPROJ, 128], [128 * DPROJ, 8], [1, wcols]]))),
                            writes=[Rw], dma=True)
                    m = 64 if ot == OT_DT else 128
                    for tc in range(4):
                        p = pA if tc % 2 == 0 else pB
                        Rp = "pA" if tc % 2 == 0 else "pB"

                        def mmg(e, wbj=wbj, p=p, m=m, tc=tc):
                            for k in range(KD):
                                ins = e.matmul(p[0:m, :], wbj[0:128, k * 128: k * 128 + m],
                                               hbuf[:, k * T + tc * 512: k * T + (tc + 1) * 512],
                                               start=(k == 0), stop=(k == KD - 1))
                            return ins
                        sc.add("pe", mmg, reads=[Rw, "hbuf"], writes=[Rp])
                        idx = j * 4 + tc
                        sg = stg[idx % 4]
                        Rs = f"stg{idx%4}"
                        if ot == OT_DT:
                            sc.add("act", (lambda e, p=p, tc=tc: e.copy(
                                stf[0:64, tc * 512:(tc + 1) * 512], p[0:64, :])),
                                reads=[Rp], writes=["stf"])
                            continue
                        if ot < OT_X:
                            sc.add("act", (lambda e, sg=sg, p=p: e.activation(sg[:], p[:], AF.Silu)),
                                   reads=[Rp], writes=[Rs])
                            dst, row = dsts[0], ot * 128
                        else:
                            sc.add("act", (lambda e, sg=sg, p=p: e.copy(sg[:], p[:])),
                                   reads=[Rp], writes=[Rs])
                            if ot < OT_B:
                                dst, row = dsts[1], (ot - OT_X) * 128
                            elif ot < OT_C:
                                dst, row = dsts[2], (ot - OT_B) * 128
                            else:
                                dst, row = dsts[3], (ot - OT_C) * 128
                        sc.add("sp", (lambda e, dst=dst, row=row, tc=tc, sg=sg: e.dma_start(
                            out=dst[row:row + 128, tc * 512:(tc + 1) * 512], in_=sg[:])),
                            reads=[Rs], dma=True)
                sc.end()

        def emit_dtprep(tag, stf, a_dst, ldt_dst):
            with ExitStack() as stP:
                sb = lambda n, s, d: stP.enter_context(nc.sbuf_tensor(n, s, d))
                t0 = sb(f"{tag}_t0", [NH, T], F32)
                t1 = sb(f"{tag}_t1", [NH, T], F32)
                sc.begin()
                sc.add("dve", lambda e: e.tensor_scalar(t0[:], stf[:], dtb_sb[:], None, op0=ALU.add),
                       reads=["stf", "dtb"], writes=["t0"])
                sc.add("act", lambda e: e.activation(t1[:], t0[:], AF.Abs), reads=["t0"], writes=["t1"])
                sc.add("act", lambda e: e.activation(t1[:], t1[:], AF.Exp, scale=-1.0), reads=["t1"], writes=["t1"])
                sc.add("act", lambda e: e.activation(t1[:], t1[:], AF.Ln, bias=1.0), reads=["t1"], writes=["t1"])
                sc.add("dve", lambda e: e.tensor_scalar_max(stf[:], t0[:], 0.0), reads=["t0"], writes=["stf"])
                sc.add("dve", lambda e: e.tensor_add(stf[:], stf[:], t1[:]), reads=["stf", "t1"], writes=["stf"])
                sc.add("dve", (lambda e: e.tensor_scalar(a_dst[:], stf[:], nega_sb[:], -1.0,
                                                         op0=ALU.mult, op1=ALU.mult)),
                       reads=["stf", "nega"], writes=["a_dst"])
                sc.add("act", lambda e: e.activation(ldt_dst[:], stf[:], AF.Ln), reads=["stf"], writes=["ldt_dst"])
                sc.end()

        def emit_scan(tag, full, a_sb, ldt_sb, src_x, src_B, state):
            with ExitStack() as stP:
                sb = lambda n, s, d: stP.enter_context(nc.sbuf_tensor(n, s, d))
                xcf = [sb(f"{tag}_xcf{i}", [128, 8 * 128], BF16) for i in range(2)]
                Bcf = [[sb(f"{tag}_Bcf{p}{i}", [64, 8 * 128], BF16) for i in range(2)]
                       for p in range(2)]
                xct = sb(f"{tag}_xct", [128, 8 * 128], BF16)
                Bct = sb(f"{tag}_Bct", [128, 8 * 128], BF16)
                aT = sb(f"{tag}_aT", [128, 64], F32)
                ldtT = sb(f"{tag}_ldtT", [128, 64], F32)
                caT = sb(f"{tag}_caT", [128, 64], F32)
                colB = sb(f"{tag}_colB", [128, 64], F32)
                calast = sb(f"{tag}_cal", [1, 64], F32)
                dec_b = sb(f"{tag}_dec", [1, 64], BF16)
                dec_cols = sb(f"{tag}_decc", [NH, NH], F32)
                wall = sb(f"{tag}_wall", [128, 64], F32)
                btil = [sb(f"{tag}_btl{i}", [128, 256], BF16) for i in range(2)]
                if full:
                    ca_fm = sb(f"{tag}_cafm", [64, 128], F32)
                    Cc = [[sb(f"{tag}_Cc{p}{i}", [64, 32 * 128], BF16) for i in range(2)]
                          for p in range(2)]
                    zc = [sb(f"{tag}_zc{i}", [128, 32 * 128], BF16) for i in range(2)]
                    cbc = [sb(f"{tag}_cbc{p}", [64, 32 * 128], BF16) for p in range(2)]
                    chat = [sb(f"{tag}_chat{p}", [64, 32 * 128], BF16) for p in range(2)]
                    e4 = sb(f"{tag}_e4", [128, 512], BF16)
                    p4 = [sb(f"{tag}_p4{i}", [128, 512], BF16) for i in range(2)]
                    dpx = sb(f"{tag}_dpx", [128, 32 * 128], BF16)
                    ypre = sb(f"{tag}_ypre", [128, 32 * 128], BF16)
                    yf = sb(f"{tag}_yf", [128, 32 * 128], BF16)
                    cam = sb(f"{tag}_cam", [1, 2048], F32)
                    expca = sb(f"{tag}_expca", [64, 128], F32)
                    grow = sb(f"{tag}_grow", [1, 128], F32)
                    st_bf = sb(f"{tag}_stbf", [NH, NH * 64], BF16)

                def loads(c):
                    cb = c % 2
                    for half in range(2):
                        sc.add("sp", (lambda e, c=c, cb=cb, half=half: e.dma_start(
                            out=xcf[cb][:, half * 512:(half + 1) * 512],
                            in_=dap(src_x, c * 128 + half * 4 * 128 * T,
                                    [[T, 128], [128 * T, 4], [1, 128]]))),
                            writes=[f"xcf{cb}"], dma=True)
                    for par in range(2):
                        sc.add("sp", (lambda e, c=c, cb=cb, par=par: e.dma_start(
                            out=Bcf[par][cb][:],
                            in_=dap(src_B, c * 128 + par * 64 * T,
                                    [[T, 64], [128 * T, 8], [1, 128]]))),
                            writes=[f"Bcf{par}{cb}"], dma=True)
                    if full:
                        for par in range(2):
                            for hf in range(2):
                                sc.add("sp", (lambda e, c=c, cb=cb, par=par, hf=hf: e.dma_start(
                                    out=Cc[par][cb][:, hf * 2048:(hf + 1) * 2048],
                                    in_=dap(d_C, c * 128 + par * 64 * T + hf * 16 * 128 * T,
                                            [[T, 64], [128 * T, 16], [1, 128]]))),
                                    writes=[f"Cc{par}{cb}"], dma=True)
                        for qr in range(4):
                            sc.add("sp", (lambda e, c=c, cb=cb, qr=qr: e.dma_start(
                                out=zc[cb][:, qr * 1024:(qr + 1) * 1024],
                                in_=dap(d_zsil, c * 128 + qr * 8 * 128 * T,
                                        [[T, 128], [128 * T, 8], [1, 128]]))),
                                writes=[f"zc{cb}"], dma=True)

                sc.begin(tag + '_pre')
                loads(0)
                sc.end()

                for c in range(NT):
                    cb = c % 2
                    sc.begin(f'{tag}_c{c}')
                    if c + 1 < NT:
                        loads(c + 1)
                    for k in range(8):
                        slot = k % 2
                        sc.add("pe", (lambda e, cb=cb, k=k, slot=slot: e.transpose(
                            pT2[slot][:, 0:128], xcf[cb][:, k * 128:(k + 1) * 128], ident_b[:])),
                            reads=[f"xcf{cb}", "ident_b"], writes=[f"pT{slot}"])
                        sc.add("dve", (lambda e, k=k, slot=slot: e.tensor_copy(
                            xct[:, k * 128:(k + 1) * 128], pT2[slot][:, 0:128])),
                            reads=[f"pT{slot}"], writes=["xct"])
                    for g in range(16):
                        par, jj = g % 2, g // 2
                        slot = g % 2
                        sc.add("pe", (lambda e, cb=cb, par=par, jj=jj, slot=slot: e.transpose(
                            pT2[slot][0:128, 0:64],
                            Bcf[par][cb][0:64, jj * 128:(jj + 1) * 128], ident_b[0:64, 0:64])),
                            reads=[f"Bcf{par}{cb}", "ident_b"], writes=[f"pT{slot}"])
                        sc.add("dve", (lambda e, g=g, slot=slot: e.tensor_copy(
                            Bct[:, g * 64:(g + 1) * 64], pT2[slot][0:128, 0:64])),
                            reads=[f"pT{slot}"], writes=["Bct"])
                    sc.add("pe", (lambda e, c=c: e.transpose(pF[0:128, 0:64],
                                                             a_sb[0:64, c * 128:(c + 1) * 128],
                                                             ident_f[0:64, 0:64])),
                           reads=["a_sb", "ident_f"], writes=["pF"])
                    sc.add("dve", lambda e: e.tensor_copy(aT[:, 0:64], pF[0:128, 0:64]),
                           reads=["pF"], writes=["aT"])
                    sc.add("pe", (lambda e, c=c: e.transpose(pF[0:128, 0:64],
                                                             ldt_sb[0:64, c * 128:(c + 1) * 128],
                                                             ident_f[0:64, 0:64])),
                           reads=["ldt_sb", "ident_f", "aT"], writes=["pF"])
                    sc.add("dve", lambda e: e.tensor_copy(ldtT[:, 0:64], pF[0:128, 0:64]),
                           reads=["pF"], writes=["ldtT"])
                    sc.add("pe", lambda e: e.matmul(pG[0:128, 0:64], tri_f[:], aT[:, 0:64], start=True, stop=True),
                           reads=["tri_f", "aT"], writes=["pG"])
                    sc.add("dve", lambda e: e.tensor_copy(caT[:, 0:64], pG[0:128, 0:64]),
                           reads=["pG"], writes=["caT"])
                    sc.add("dve", lambda e: e.tensor_sub(colB[:, 0:64], ldtT[:, 0:64], caT[:, 0:64]),
                           reads=["ldtT", "caT"], writes=["colB"])
                    sc.add("act", lambda e: e.dma_start(out=calast[:], in_=caT[127:128, 0:64]),
                           reads=["caT"], writes=["calast"], dma=True)
                    sc.add("act", lambda e: e.activation(dec_b[:], calast[:], AF.Exp),
                           reads=["calast"], writes=["dec_b"])

                    def wallmm(e):
                        e.matmul(pG[0:128, 64:128], ones_f[:], calast[:], start=True, stop=False)
                        return e.matmul(pG[0:128, 64:128], ident_f[:], colB[:, 0:64], start=False, stop=True)
                    sc.add("pe", wallmm, reads=["ones_f", "calast", "ident_f", "colB"], writes=["pG"])
                    sc.add("act", lambda e: e.activation(wall[:, 0:64], pG[0:128, 64:128], AF.Exp),
                           reads=["pG"], writes=["wall"])
                    sc.add("pe", lambda e: e.matmul(pG[0:NH, 128:192], ones_b[0:1, 0:NH], dec_b[:], start=True, stop=True),
                           reads=["ones_b", "dec_b"], writes=["pG"])
                    sc.add("dve", lambda e: e.tensor_copy(dec_cols[:, 0:NH], pG[0:NH, 128:192]),
                           reads=["pG"], writes=["dec_cols"])

                    if full:
                        sc.add("pe", (lambda e: e.transpose(pF[0:64, 0:128], caT[0:128, 0:64], ident_f[:])),
                               reads=["caT", "ident_f", "ldtT"], writes=["pF"])
                        sc.add("dve", lambda e: e.tensor_copy(ca_fm[:], pF[0:64, 0:128]),
                               reads=["pF"], writes=["ca_fm"])
                        sc.add("act", lambda e: e.activation(expca[:], ca_fm[:], AF.Exp),
                               reads=["ca_fm"], writes=["expca"])
                        sc.add("pool", (lambda e, c=c: e.dma_start(
                            out=d_expca[0:64, c * 128:(c + 1) * 128], in_=expca[:])),
                            reads=["expca"], writes=["d_expca"], dma=True)
                        for par in range(2):
                            sc.add("pool", (lambda e, c=c, par=par: e.dma_start(
                                out=cbc[par][:], in_=dap(d_expca, par * T + c * 128,
                                                         [[0, 64], [2 * T, 32], [1, 128]]))),
                                reads=["d_expca"], writes=[f"cbc{par}"], dma=True)
                            sc.add("dve", (lambda e, cb=cb, par=par: e.tensor_mul(
                                chat[par][:], Cc[par][cb][:], cbc[par][:])),
                                reads=[f"Cc{par}{cb}", f"cbc{par}"], writes=[f"chat{par}"])
                        sc.add("dve", lambda e: e.tensor_copy(st_bf[:], state[:]),
                               reads=["state"], writes=["st_bf"])

                    for q in range(16):
                        hs = [4 * q + i for i in range(4)]
                        g = q
                        gb, gt = 64 * (g % 2), (g // 2) * 128
                        if full:
                            if q % 4 == 0:
                                aa = q // 4
                                sc.add("pool", (lambda e, aa=aa: e.dma_start(
                                    out=cam[:], in_=ca_fm[aa * 16:(aa + 1) * 16, :])),
                                    reads=["ca_fm"], writes=["cam"], dma=True)

                            def d2mm(e, q=q):
                                e.matmul(pA[:], ones_f[:], cam[0:1, (q % 4) * 512:(q % 4 + 1) * 512],
                                         start=True, stop=False)
                                return e.matmul(pA[:], ident_b[:], trineg4_b[:], start=False, stop=True)
                            sc.add("pe", d2mm, reads=["cam", "ones_f", "ident_b", "trineg"], writes=["pA"])

                            def spmm(e, hs=hs, g=g, cb=cb):
                                for i, h in enumerate(hs):
                                    ins = e.matmul(pB[:, i * 128:(i + 1) * 128],
                                                   Bcf[g % 2][cb][0:64, (g // 2) * 128:(g // 2) * 128 + 128],
                                                   Cc[h % 2][cb][0:64, (h // 2) * 128:(h // 2) * 128 + 128],
                                                   start=True, stop=True)
                                return ins
                            sc.add("pe", spmm,
                                   reads=[f"Bcf{g%2}{cb}", f"Cc0{cb}", f"Cc1{cb}"], writes=["pB"])
                            for i, h in enumerate(hs):
                                sc.add("act", (lambda e, i=i, h=h: e.activation(
                                    e4[:, i * 128:(i + 1) * 128], pA[:, i * 128:(i + 1) * 128],
                                    AF.Exp, bias=colB[:, h:h + 1])),
                                    reads=["pA", "colB"], writes=["e4"])
                            p4q = p4[q % 2]
                            sc.add("dve", (lambda e, p4q=p4q: e.tensor_mul(p4q[:], pB[:], e4[:])),
                                   reads=["pB", "e4"], writes=[f"p4_{q%2}"])

                            def ymm(e, hs=hs, g=g, p4q=p4q):
                                for i, h in enumerate(hs):
                                    bp = 64 * (h % 2)
                                    col = ((h % 8) // 2) * 128
                                    e.matmul(pC[bp:bp + 64, col:col + 128],
                                             xct[:, g * 64:g * 64 + 64],
                                             p4q[:, i * 128:(i + 1) * 128],
                                             start=True, stop=False, tile_position=(0, bp))
                                    ins = e.matmul(pC[bp:bp + 64, col:col + 128],
                                                   st_bf[0:64, h * 64:(h + 1) * 64],
                                                   chat[h % 2][0:64,
                                                               (h // 2) * 128:(h // 2) * 128 + 128],
                                                   start=False, stop=True, tile_position=(0, bp))
                                return ins
                            sc.add("pe", ymm, reads=[f"p4_{q%2}", "xct", "st_bf", "chat0", "chat1"],
                                   writes=["pC"])
                            if q % 2 == 1:
                                def dpxb(e, q=q, cb=cb):
                                    for h in range(4 * q - 4, 4 * q + 4):
                                        gg = h // 4
                                        bp = 64 * (h % 2)
                                        col = (h // 2) * 128
                                        xb_ = 64 * (gg % 2)
                                        xtc = (gg // 2) * 128
                                        ins = e.tensor_scalar(dpx[bp:bp + 64, col:col + 128],
                                                              xcf[cb][xb_:xb_ + 64, xtc:xtc + 128],
                                                              dpcols[:, h:h + 1], None, op0=ALU.mult)
                                    return ins
                                sc.add("dve", dpxb, reads=[f"xcf{cb}", "dpcols"], writes=["dpx"])
                                a = q // 2
                                sc.add("dve", (lambda e, a=a: e.tensor_add(
                                    ypre[:, a * 512:(a + 1) * 512], dpx[:, a * 512:(a + 1) * 512], pC[:])),
                                    reads=["dpx", "pC"], writes=["ypre"])
                        btq = btil[q % 2]

                        def btb(e, hs=hs, g=g, btq=btq):
                            for i, h in enumerate(hs):
                                ins = e.tensor_scalar(btq[:, i * 64:(i + 1) * 64],
                                                      Bct[:, g * 64:g * 64 + 64],
                                                      wall[:, h:h + 1], None, op0=ALU.mult)
                            return ins
                        sc.add("dve", btb, reads=["Bct", "wall"], writes=[f"btil{q%2}"])

                        def saddmm(e, hs=hs, g=g, btq=btq):
                            for i, h in enumerate(hs):
                                ins = e.matmul(pD[0:64, (h % 8) * 64:(h % 8) * 64 + 64],
                                               btq[:, i * 64:(i + 1) * 64],
                                               xct[:, g * 64:g * 64 + 64],
                                               start=True, stop=True)
                            return ins
                        sc.add("pe", saddmm, reads=[f"btil{q%2}", "xct"], writes=["pD"])
                        if q % 2 == 1:
                            def stt(e, q=q):
                                for h in range(4 * q - 4, 4 * q + 4):
                                    ins = e.scalar_tensor_tensor(
                                        state[0:64, h * 64:(h + 1) * 64],
                                        state[0:64, h * 64:(h + 1) * 64],
                                        dec_cols[:, h:h + 1],
                                        pD[0:64, (h % 8) * 64:(h % 8) * 64 + 64],
                                        op0=ALU.mult, op1=ALU.add)
                                return ins
                            sc.add("dve", stt, reads=["pD", "dec_cols", "state"], writes=["state"])

                    if full:
                        sc.add("dve", (lambda e, cb=cb: e.tensor_mul(yf[:], ypre[:], zc[cb][:])),
                               reads=["ypre", f"zc{cb}"], writes=["yf"])
                        sc.add("dve", lambda e: e.tensor_mul(dpx[:], yf[:], yf[:]),
                               reads=["yf"], writes=["dpx"])

                        def ssmm(e):
                            for k in range(32):
                                ins = e.matmul(pG[0:1, 192:320], onescol_b[:],
                                               dpx[:, k * 128:(k + 1) * 128],
                                               start=(k == 0), stop=(k == 31))
                            return ins
                        sc.add("pe", ssmm, reads=["dpx", "onescol"], writes=["pG"])
                        sc.add("act", lambda e: e.activation(grow[:], pG[0:1, 192:320], AF.Ln,
                                                             scale=1.0 / DIN, bias=EPS),
                               reads=["pG"], writes=["grow"])
                        sc.add("act", lambda e: e.activation(grow[:], grow[:], AF.Exp, scale=-0.5),
                               reads=["grow"], writes=["grow"])
                        sc.add("pe", lambda e: e.matmul(pG[0:128, 320:321], grow[:], ones_f[0:1, 0:1],
                                                        start=True, stop=True),
                               reads=["grow", "ones_f"], writes=["pG"])
                        sc.add("dve", (lambda e, c=c: e.tensor_copy(g_cols[:, c:c + 1], pG[0:128, 320:321])),
                               reads=["pG"], writes=["g_cols"])
                        for qr in range(4):
                            sc.add("pool", (lambda e, c=c, qr=qr: e.dma_start(
                                out=dap(d_yf, c * 128 + qr * 8 * 128 * T, [[T, 128], [128 * T, 8], [1, 128]]),
                                in_=yf[:, qr * 1024:(qr + 1) * 1024])),
                                reads=["yf"], writes=["d_yf"], dma=True)
                    sc.end()

        def emit_mlp(hbuf):
            with ExitStack() as stP:
                sb = lambda n, s, d: stP.enter_context(nc.sbuf_tensor(n, s, d))
                wb = [sb(f"m_wb{i}", [128, KD * 128], BF16) for i in range(2)]
                t0 = [sb(f"m_t0{i}", [128, 512], F32) for i in range(2)]
                t1 = [sb(f"m_t1{i}", [128, 512], F32) for i in range(2)]
                stg = [sb(f"m_st{i}", [128, 512], BF16) for i in range(4)]
                fb = sb("m_fb", [128, 64], F32)
                sc.begin('mlp')
                sc.add("sp", lambda e: e.dma_start(out=fb[:], in_=fc1b_c.ap()), writes=["fb"], dma=True)
                for j in range(64):
                    wbj = wb[j % 2]
                    Rw = f"mwb{j%2}"
                    for half in range(2):
                        sc.add("pool", (lambda e, wbj=wbj, j=j, half=half: e.dma_start(
                            out=wbj[:, half * 1024:(half + 1) * 1024],
                            in_=dap(fc1T, j * 128 + half * 8 * 128 * INTER,
                                    [[INTER, 128], [128 * INTER, 8], [1, 128]]))),
                            writes=[Rw], dma=True)
                    for tc in range(4):
                        p = pA if tc % 2 == 0 else pB
                        Rp = "pA" if tc % 2 == 0 else "pB"

                        def mmg(e, wbj=wbj, p=p, tc=tc):
                            for k in range(KD):
                                ins = e.matmul(p[:], wbj[:, k * 128:(k + 1) * 128],
                                               hbuf[:, k * T + tc * 512: k * T + (tc + 1) * 512],
                                               start=(k == 0), stop=(k == KD - 1))
                            return ins
                        sc.add("pe", mmg, reads=[Rw, "hbuf"], writes=[Rp])
                        idx = j * 4 + tc
                        tt0, tt1, sg = t0[idx % 2], t1[idx % 2], stg[idx % 4]
                        R0, R1, Rs = f"mt0{idx%2}", f"mt1{idx%2}", f"mst{idx%4}"
                        sc.add("act", (lambda e, tt0=tt0, p=p, j=j: e.activation(
                            tt0[:], p[:], AF.Identity, bias=fb[:, j:j + 1])),
                            reads=[Rp, "fb"], writes=[R0])
                        sc.add("dve", (lambda e, tt0=tt0, tt1=tt1: e.tensor_mul(tt1[:], tt0[:], tt0[:])),
                               reads=[R0], writes=[R1])
                        sc.add("dve", (lambda e, tt1=tt1: e.tensor_scalar(tt1[:], tt1[:], 0.044715, 1.0,
                                                                          op0=ALU.mult, op1=ALU.add)),
                               reads=[R1], writes=[R1])
                        sc.add("dve", (lambda e, tt0=tt0, tt1=tt1: e.tensor_mul(tt1[:], tt1[:], tt0[:])),
                               reads=[R0, R1], writes=[R1])
                        sc.add("act", (lambda e, tt1=tt1: e.activation(tt1[:], tt1[:], AF.Sigmoid, scale=GELU_K)),
                               reads=[R1], writes=[R1])
                        sc.add("dve", (lambda e, sg=sg, tt0=tt0, tt1=tt1: e.tensor_mul(sg[:], tt1[:], tt0[:])),
                               reads=[R0, R1], writes=[Rs])
                        sc.add("sp", (lambda e, sg=sg, j=j, tc=tc: e.dma_start(
                            out=d_u[j * 128:(j + 1) * 128, tc * 512:(tc + 1) * 512], in_=sg[:])),
                            reads=[Rs], dma=True)
                sc.end()

        def emit_p23b():
            with ExitStack() as stP:
                sb = lambda n, s, d: stP.enter_context(nc.sbuf_tensor(n, s, d))
                wq32 = sb("o_w32", [128, 8 * 512], F32)
                wqb = sb("o_wqb", [128, 32 * 512], BF16)
                f2b = sb("o_f2b", [128, 64 * 512], BF16)
                nw_sb = sb("o_nw", [128, 32], F32)
                yfc = [sb(f"o_yf{i}", [128, 32 * 128], BF16) for i in range(3)]
                uc = [sb(f"o_uc{i}", [128, 64 * 128], BF16) for i in range(3)]
                rsb = [sb(f"o_rs{i}", [128, 512], F32) for i in range(3)]
                ost = [sb(f"o_ost{i}", [128, 512], F32) for i in range(2)]
                sc.begin()
                sc.add("sp", lambda e: e.dma_start(out=nw_sb[:], in_=nw_cols.ap()), writes=["nw"], dma=True)
                sc.end()
                for qq in range(4):
                    sc.begin(f'p23b_q{qq}')
                    for gset in range(4):
                        sc.add("sp", (lambda e, qq=qq, gset=gset: e.dma_start(
                            out=wq32[:], in_=dap(woutT, qq * 512 + gset * 8 * 128 * D,
                                                 [[D, 128], [128 * D, 8], [1, 512]]))),
                            writes=["wq32"], dma=True)
                        for k in range(8):
                            kk = gset * 8 + k
                            sc.add("dve", (lambda e, k=k, kk=kk: e.tensor_scalar(
                                wqb[:, kk * 512:(kk + 1) * 512], wq32[:, k * 512:(k + 1) * 512],
                                nw_sb[:, kk:kk + 1], None, op0=ALU.mult)),
                                reads=["wq32", "nw"], writes=["wqb"])
                    for gset in range(8):
                        sc.add("pool", (lambda e, qq=qq, gset=gset: e.dma_start(
                            out=f2b[:, gset * 4096:(gset + 1) * 4096],
                            in_=dap(fc2T, qq * 512 + gset * 8 * 128 * D,
                                    [[D, 128], [128 * D, 8], [1, 512]]))),
                            writes=["f2b"], dma=True)
                    for i in range(NT):
                        ib = i % 3
                        for gset in range(4):
                            sc.add("sp", (lambda e, i=i, ib=ib, gset=gset: e.dma_start(
                                out=yfc[ib][:, gset * 1024:(gset + 1) * 1024],
                                in_=dap(d_yf, i * 128 + gset * 8 * 128 * T,
                                        [[T, 128], [128 * T, 8], [1, 128]]))),
                                writes=[f"yfc{ib}"], dma=True)
                        for gset in range(8):
                            qeng = "act" if gset % 2 == 0 else "sp"
                            sc.add(qeng, (lambda e, i=i, ib=ib, gset=gset: e.dma_start(
                                out=uc[ib][:, gset * 1024:(gset + 1) * 1024],
                                in_=dap(d_u, i * 128 + gset * 8 * 128 * T,
                                        [[T, 128], [128 * T, 8], [1, 128]]))),
                                writes=[f"uc{ib}"], dma=True)
                        sc.add("sp", (lambda e, i=i, ib=ib, qq=qq: e.dma_start(
                            out=rsb[ib][:], in_=d_res[i * 128:(i + 1) * 128, qq * 512:(qq + 1) * 512])),
                            reads=[f"dres{i}"], writes=[f"rsb{ib}"], dma=True)

                        def opmm(e, ib=ib):
                            for k in range(32):
                                ins = e.matmul(pA[:], yfc[ib][:, k * 128:(k + 1) * 128],
                                               wqb[:, k * 512:(k + 1) * 512],
                                               start=(k == 0), stop=(k == 31))
                            return ins
                        sc.add("pe", opmm, reads=[f"yfc{ib}", "wqb"], writes=["pA"])

                        def fcmm(e, ib=ib):
                            for k in range(64):
                                ins = e.matmul(pB[:], uc[ib][:, k * 128:(k + 1) * 128],
                                               f2b[:, k * 512:(k + 1) * 512],
                                               start=(k == 0), stop=(k == 63))
                            return ins
                        sc.add("pe", fcmm, reads=[f"uc{ib}", "f2b"], writes=["pB"])
                        io = i % 2
                        oi = ost[io]
                        sc.add("dve", (lambda e, oi=oi, i=i: e.tensor_scalar(
                            oi[:], pA[:], g_cols[:, i:i + 1], None, op0=ALU.mult)),
                            reads=["pA", "g_cols"], writes=[f"ost{io}"])
                        sc.add("dve", (lambda e, oi=oi: e.tensor_add(oi[:], oi[:], pB[:])),
                               reads=["pB", f"ost{io}"], writes=[f"ost{io}"])
                        sc.add("dve", (lambda e, oi=oi, ib=ib: e.tensor_add(oi[:], oi[:], rsb[ib][:])),
                               reads=[f"rsb{ib}", f"ost{io}"], writes=[f"ost{io}"])
                        sc.add("sp", (lambda e, oi=oi, i=i, qq=qq: e.dma_start(
                            out=d_res[i * 128:(i + 1) * 128, qq * 512:(qq + 1) * 512], in_=oi[:])),
                            reads=[f"ost{io}"], writes=[f"dres{i}"], dma=True, stream=f"ost{io}")
                    sc.end()

        def emit_gateup(hbuf):
            with ExitStack() as stP:
                sb = lambda n, s, d: stP.enter_context(nc.sbuf_tensor(n, s, d))
                wgb = [sb(f"f_wg{i}", [128, KD * 128], BF16) for i in range(2)]
                wub = [sb(f"f_wu{i}", [128, KD * 128], BF16) for i in range(2)]
                gsb = [sb(f"f_g{i}", [128, 512], BF16) for i in range(2)]
                usb = [sb(f"f_u{i}", [128, 512], BF16) for i in range(2)]
                fsb = [sb(f"f_f{i}", [128, 512], BF16) for i in range(4)]
                sgc = sb("f_sg", [128, 64], F32)
                suc = sb("f_su", [128, 64], F32)
                sc.begin('p5a')
                sc.add("sp", lambda e: e.dma_start(out=sgc[:], in_=sg_c.ap()), writes=["sgc"], dma=True)
                sc.add("sp", lambda e: e.dma_start(out=suc[:], in_=su_c.ap()), writes=["suc"], dma=True)
                for j in range(64):
                    jb = j % 2
                    for (wt, dst, Rw) in ((wg_i, wgb[jb], f"fwg{jb}"), (wu_i, wub[jb], f"fwu{jb}")):
                        for half in range(2):
                            sc.add("pool", (lambda e, wt=wt, dst=dst, j=j, half=half: e.dma_start(
                                out=dst[:, half * 1024:(half + 1) * 1024],
                                in_=dap(wt, j * 128 + half * 8 * 128 * INTER,
                                        [[INTER, 128], [128 * INTER, 8], [1, 128]]))),
                                writes=[Rw], dma=True)
                    for tc in range(4):
                        def gmm(e, jb=jb, tc=tc):
                            for k in range(KD):
                                ins = e.matmul(pA[:], wgb[jb][:, k * 128:(k + 1) * 128],
                                               hbuf[:, k * T + tc * 512:k * T + (tc + 1) * 512],
                                               start=(k == 0), stop=(k == KD - 1))
                            return ins
                        sc.add("pe", gmm, reads=[f"fwg{jb}", "hbuf"], writes=["pA"])

                        def umm(e, jb=jb, tc=tc):
                            for k in range(KD):
                                ins = e.matmul(pB[:], wub[jb][:, k * 128:(k + 1) * 128],
                                               hbuf[:, k * T + tc * 512:k * T + (tc + 1) * 512],
                                               start=(k == 0), stop=(k == KD - 1))
                            return ins
                        sc.add("pe", umm, reads=[f"fwu{jb}", "hbuf"], writes=["pB"])
                        idx = j * 4 + tc
                        gg, uu, ff = gsb[idx % 2], usb[idx % 2], fsb[idx % 4]
                        Rg, Ru, Rf = f"fg{idx%2}", f"fu{idx%2}", f"ff{idx%4}"
                        sc.add("act", (lambda e, gg=gg, j=j: e.activation(gg[:], pA[:], AF.Silu,
                                                                         scale=sgc[:, j:j + 1])),
                               reads=["pA", "sgc"], writes=[Rg])
                        sc.add("act", (lambda e, uu=uu, j=j: e.activation(uu[:], pB[:], AF.Identity,
                                                                         scale=suc[:, j:j + 1])),
                               reads=["pB", "suc"], writes=[Ru])
                        sc.add("dve", (lambda e, ff=ff, gg=gg, uu=uu: e.tensor_mul(ff[:], gg[:], uu[:])),
                               reads=[Rg, Ru], writes=[Rf])
                        sc.add("sp", (lambda e, ff=ff, j=j, tc=tc: e.dma_start(
                            out=d_fuse[j * 128:(j + 1) * 128, tc * 512:(tc + 1) * 512], in_=ff[:])),
                            reads=[Rf], dma=True)
                sc.end()

        def emit_down():
            with ExitStack() as stP:
                sb = lambda n, s, d: stP.enter_context(nc.sbuf_tensor(n, s, d))
                wdb = sb("d_wd", [128, 64 * 512], BF16)
                fcol = [sb(f"d_f{i}", [128, 64 * 128], BF16) for i in range(3)]
                rsb = [sb(f"d_rs{i}", [128, 512], F32) for i in range(3)]
                ostg = [sb(f"d_o{i}", [128, 512], F32) for i in range(2)]
                sdb = sb("d_sd", [128, D], F32)
                sc.begin()
                sc.add("sp", lambda e: e.dma_start(out=sdb[:], in_=dap(sd_r, 0, [[0, 128], [1, D]])),
                       writes=["sdb"], dma=True)
                sc.end()
                for qq in range(4):
                    sc.begin(f'p5b_q{qq}')
                    for gset in range(8):
                        sc.add("pool", (lambda e, qq=qq, gset=gset: e.dma_start(
                            out=wdb[:, gset * 4096:(gset + 1) * 4096],
                            in_=dap(wd_i, qq * 512 + gset * 8 * 128 * D,
                                    [[D, 128], [128 * D, 8], [1, 512]]))),
                            writes=["wdb"], dma=True)
                    for i in range(NT):
                        ib = i % 3
                        for gset in range(8):
                            qeng = "act" if gset % 2 == 0 else "sp"
                            sc.add(qeng, (lambda e, i=i, ib=ib, gset=gset: e.dma_start(
                                out=fcol[ib][:, gset * 1024:(gset + 1) * 1024],
                                in_=dap(d_fuse, i * 128 + gset * 8 * 128 * T,
                                        [[T, 128], [128 * T, 8], [1, 128]]))),
                                writes=[f"fcol{ib}"], dma=True)
                        sc.add("sp", (lambda e, i=i, ib=ib, qq=qq: e.dma_start(
                            out=rsb[ib][:], in_=d_res[i * 128:(i + 1) * 128, qq * 512:(qq + 1) * 512])),
                            writes=[f"drs{ib}"], dma=True)

                        def dmm(e, ib=ib):
                            for k in range(64):
                                ins = e.matmul(pA[:], fcol[ib][:, k * 128:(k + 1) * 128],
                                               wdb[:, k * 512:(k + 1) * 512],
                                               start=(k == 0), stop=(k == 63))
                            return ins
                        sc.add("pe", dmm, reads=[f"fcol{ib}", "wdb"], writes=["pA"])
                        io = i % 2
                        oo = ostg[io]
                        sc.add("dve", (lambda e, oo=oo, qq=qq: e.tensor_mul(
                            oo[:], pA[:], sdb[:, qq * 512:(qq + 1) * 512])),
                            reads=["pA", "sdb"], writes=[f"dost{io}"])
                        sc.add("dve", (lambda e, oo=oo, ib=ib: e.tensor_add(oo[:], oo[:], rsb[ib][:])),
                               reads=[f"drs{ib}", f"dost{io}"], writes=[f"dost{io}"])
                        sc.add("sp", (lambda e, oo=oo, i=i, qq=qq: e.dma_start(
                            out=out[i * 128:(i + 1) * 128, qq * 512:(qq + 1) * 512], in_=oo[:])),
                            reads=[f"dost{io}"], dma=True)
                    sc.end()

        with ExitStack() as stO:
            sbO = lambda n, s, d: stO.enter_context(nc.sbuf_tensor(n, s, d))
            a_my = sbO("a_my", [NH, T], F32)
            ldt_my = sbO("ldt_my", [NH, T], F32)
            a_pp = sbO("a_pp", [NH, T], F32)
            ldt_pp = sbO("ldt_pp", [NH, T], F32)
            state = sbO("state", [NH, NH * 64], F32)

            with ExitStack() as stH:
                hbuf = stH.enter_context(nc.sbuf_tensor("hbuf1", [128, KD * T], BF16))
                stf = stH.enter_context(nc.sbuf_tensor("stf1", [NH, T], F32))
                if _on("P0"):
                    emit_hpass("p0", hbuf, xb.ap(), 0, affine=True, res_add=True)
                if _on("P1"):
                    emit_inproj("p1", hbuf, list(range(0, 80)) + [OT_DT],
                                [d_zsil, d_xf, d_Bf, d_C], stf)
                    emit_dtprep("dpm", stf, a_my, ldt_my)
                if _on("spill"):
                    sc.begin()
                    sc.add("sp", lambda e: e.dma_start(out=d_h.ap(), in_=hbuf[:]),
                           reads=["hbuf"], dma=True)
                    sc.end()

            with ExitStack() as stH:
                hbuf = stH.enter_context(nc.sbuf_tensor("hbuf2", [128, KD * T], BF16))
                stf = stH.enter_context(nc.sbuf_tensor("stf2", [NH, T], F32))
                if _on("P0p"):
                    emit_hpass("pp", hbuf, xb.ap(), T, affine=True, res_add=False)
                if _on("P15a"):
                    emit_inproj("pq", hbuf, list(range(OT_X, OT_C)) + [OT_DT],
                                [None, d_xpf, d_Bpf, None], stf)
                    emit_dtprep("dpp", stf, a_pp, ldt_pp)

            sc.begin()
            sc.add("dve", lambda e: e.memset(state[:], 0.0), writes=["state"])
            sc.end()
            if _on("P15b"):
                emit_scan("s15", False, a_pp, ldt_pp, d_xpf, d_Bpf, state)
            sc.begin()
            sc.add("dve", lambda e: e.tensor_scalar(state[:], state[:], rm_sb[:], None, op0=ALU.mult),
                   reads=["state", "rm"], writes=["state"])
            sc.end()

            if _on("P3a"):
                with ExitStack() as stH:
                    hbuf = stH.enter_context(nc.sbuf_tensor("hbuf3", [128, KD * T], BF16))
                    sc.begin()
                    sc.add("sp", lambda e: e.dma_start(out=hbuf[:], in_=d_h.ap()),
                           writes=["hbuf"], dma=True)
                    sc.end()
                    emit_mlp(hbuf)

            if _on("P2"):
                emit_scan("s2", True, a_my, ldt_my, d_xf, d_Bf, state)

        if _on("P23b"):
            emit_p23b()

        with ExitStack() as stH:
            hbuf = stH.enter_context(nc.sbuf_tensor("hbuf4", [128, KD * T], BF16))
            if _on("P4"):
                emit_hpass("p4", hbuf, d_res.ap(), 0, affine=False, res_add=False)
            if _on("P5a"):
                emit_gateup(hbuf)
        if _on("P5b"):
            emit_down()

        if taps:
            sc.begin()
            for (th, tout, shp) in taps:
                n = int(np.prod(shp))
                sc.add("sp", (lambda e, th=th, tout=tout, n=n: e.dma_start(
                    out=dap(tout, 0, [[1, n]]), in_=dap(th, 0, [[1, n]]))), dma=True)
            sc.end()

    return nc



def _consts():
    ident = np.eye(128, dtype=np.float32)
    tri = np.zeros((128, 128), np.float32)
    for s_ in range(128):
        tri[s_, s_:] = 1.0
    trineg = np.zeros((128, 128), np.float32)
    for s_ in range(128):
        trineg[s_, :s_] = -1e30
    ones = np.ones((1, 128), np.float32)
    return ident, tri, trineg, ones


def kernel(L0_attn_hidden_states, L0_attn_ln_w, L0_attn_ln_b, L0_attn_in_proj_w,
           L0_attn_dt_bias, L0_attn_A_log, L0_attn_Dp, L0_attn_norm_w,
           L0_attn_out_proj_w, L0_attn_fc1_w, L0_attn_fc1_b, L0_attn_fc2_w,
           L0_attn_fc2_b, L0_ffn_w_gate, L0_ffn_w_up, L0_ffn_w_down,
           L0_ffn_s_gate, L0_ffn_s_up, L0_ffn_s_down):
    global _last_results
    x = np.ascontiguousarray(np.asarray(L0_attn_hidden_states, dtype=np.float32))
    ident, tri, trineg, ones = _consts()
    shared = {
        "winT": np.ascontiguousarray(np.asarray(L0_attn_in_proj_w, np.float32).T),
        "woutT": np.ascontiguousarray(np.asarray(L0_attn_out_proj_w, np.float32).T),
        "fc1T": np.ascontiguousarray(np.asarray(L0_attn_fc1_w, np.float32).T),
        "fc2T": np.ascontiguousarray(np.asarray(L0_attn_fc2_w, np.float32).T),
        "lnw_r": np.ascontiguousarray(np.asarray(L0_attn_ln_w, np.float32)),
        "lnb_r": np.ascontiguousarray(np.asarray(L0_attn_ln_b, np.float32)),
        "dtb_c": np.ascontiguousarray(np.asarray(L0_attn_dt_bias, np.float32).reshape(NH, 1)),
        "alog_c": np.ascontiguousarray(np.asarray(L0_attn_A_log, np.float32).reshape(NH, 1)),
        "dp_r": np.ascontiguousarray(np.asarray(L0_attn_Dp, np.float32).reshape(1, NH)),
        "nw_cols": np.ascontiguousarray(np.asarray(L0_attn_norm_w, np.float32).reshape(32, 128).T),
        "fc1b_c": np.ascontiguousarray(np.asarray(L0_attn_fc1_b, np.float32).reshape(64, 128).T),
        "fc2b_r": np.ascontiguousarray(np.asarray(L0_attn_fc2_b, np.float32)),
        "wg_i": np.ascontiguousarray(np.asarray(L0_ffn_w_gate, np.int8)),
        "wu_i": np.ascontiguousarray(np.asarray(L0_ffn_w_up, np.int8)),
        "wd_i": np.ascontiguousarray(np.asarray(L0_ffn_w_down, np.int8)),
        "sg_c": np.ascontiguousarray(np.asarray(L0_ffn_s_gate, np.float32).reshape(64, 128).T),
        "su_c": np.ascontiguousarray(np.asarray(L0_ffn_s_up, np.float32).reshape(64, 128).T),
        "sd_r": np.ascontiguousarray(np.asarray(L0_ffn_s_down, np.float32)),
        "ident_d": ident, "tri_d": tri, "trineg_d": trineg, "ones_d": ones,
    }
    in_maps = []
    for c in range(8):
        b, H = c // 2, c % 2
        xc = x[b] if H == 0 else np.ascontiguousarray(np.roll(x[b], T, axis=0))
        m = dict(shared)
        m["xb"] = np.ascontiguousarray(xc)
        m["rm_c"] = np.full((NH, 1), 1.0 if H == 1 else 0.0, np.float32)
        in_maps.append(m)
    nc = build_nc()
    cores = list(range(8)) if _DEV_CORES is None else list(_DEV_CORES)
    res = run_bass_kernel_spmd(nc, [in_maps[c] for c in cores], core_ids=list(range(len(cores))))
    _last_results = res
    outp = np.zeros((BATCH, SEQ, D), np.float32)
    for i, c in enumerate(cores):
        b, H = c // 2, c % 2
        outp[b, H * T:(H + 1) * T, :] = res.results[i]["out"]
    return outp


# version 05
# speedup vs baseline: 1.0842x; 1.0842x over previous
import numpy as np
from contextlib import ExitStack

import concourse.bass as bass
import concourse.mybir as mybir
from concourse.bass_utils import run_bass_kernel_spmd

F32 = mybir.dt.float32
BF16 = mybir.dt.bfloat16
I8 = mybir.dt.int8
AF = mybir.ActivationFunctionType
ALU = mybir.AluOpType

D = 2048
T = 2048
SEQ = 4096
BATCH = 4
DIN = 4096
DXB = 1024
NH = 64
INTER = 8192
DPROJ = 10304
OT_X, OT_B, OT_C, OT_DT = 32, 40, 48, 80
NT = 16
KD = 16
EPS = 1e-5
GELU_K = 1.5957691216057308

_STOP_AFTER = None
_TAPS = ()
_DEV_CORES = None
_last_results = None

ENGS = ("pe", "act", "dve", "pool", "sp")
_EATTR = {"pe": "tensor", "act": "scalar", "dve": "vector", "pool": "gpsimd", "sp": "sync"}

_ORDER = ["prelude", "P0", "P1", "spill", "P0p", "P15a", "P15b",
          "P3a", "P2", "P23b", "P4", "P5a", "P5b"]


def _on(name):
    if _STOP_AFTER is None:
        return True
    return _ORDER.index(name) <= _ORDER.index(_STOP_AFTER)


def dap(t, offset, dims):
    return bass.AP(t, offset, [list(d) for d in dims])


class _Op:
    __slots__ = ("eng", "emit", "deps", "inck", "inc", "idx", "tgt", "isdma", "skey")

    def __init__(self, eng, emit, isdma, skey):
        self.eng = eng
        self.emit = emit
        self.deps = []
        self.inck = 16 if isdma else 1
        self.inc = False
        self.idx = -1
        self.tgt = None
        self.isdma = isdma
        self.skey = skey


class Sched:

    def __init__(self, nc, st):
        self.nc = nc
        self.st = st
        self.sem = {e: st.enter_context(nc.semaphore("sem_" + e)) for e in ENGS}
        self.cnt = {k: 0 for k in ENGS}
        self.blk_id = 0
        self.ops = None
        self.writer = {}
        self.readers = {}
        self.nsem = 0

    def begin(self, name=None):
        assert self.ops is None
        self.scope = name or f"b{self.blk_id}"
        self.ops = {e: [] for e in ENGS}

    def add(self, eng, emit, reads=(), writes=(), dma=False, deps=(), stream=None):
        skey = eng
        if dma:
            if stream is None:
                for r in writes:
                    stream = r
                    break
            if stream is None:
                for r in reads:
                    stream = r
                    break
            skey = (eng, stream)
            if skey not in self.sem:
                self.sem[skey] = self.st.enter_context(
                    self.nc.semaphore(f"dsem{self.nsem}"))
                self.nsem += 1
                self.cnt[skey] = 0
        op = _Op(eng, emit, dma, skey)
        dd = set()
        for r in reads:
            w = self.writer.get(r)
            if w is not None and w[0] == self.blk_id and w[1] is not op:
                dd.add(w[1])
            self.readers.setdefault(r, []).append((self.blk_id, op))
        for r in writes:
            w = self.writer.get(r)
            if w is not None and w[0] == self.blk_id:
                dd.add(w[1])
            for (b, rd) in self.readers.get(r, ()):
                if b == self.blk_id and rd is not op:
                    dd.add(rd)
            self.writer[r] = (self.blk_id, op)
            self.readers[r] = []
        for d in deps:
            if d is not None:
                dd.add(d)
        op.deps = list(dd)
        self.ops[eng].append(op)
        return op

    def end(self):
        ops = self.ops
        for e in ENGS:
            for op in ops[e]:
                if op.isdma:
                    op.inc = True
                for d in op.deps:
                    d.inc = True
        cnt = dict(self.cnt)
        for e in ENGS:
            for op in ops[e]:
                op.idx = len(ops[e])
        for e in ENGS:
            for i, op in enumerate(ops[e]):
                op.idx = i
                if op.inc:
                    cnt[op.skey] = cnt.get(op.skey, 0) + op.inck
                op.tgt = cnt.get(op.skey, 0)
        for e in ENGS:
            for op in ops[e]:
                for d in op.deps:
                    if d.eng == e:
                        assert d.idx < op.idx, "same-engine dep must precede"
        self.cnt.update(cnt)
        nc, sem = self.nc, self.sem
        with nc.named_scope(self.scope), nc.Block() as blk:
            for e in ENGS:
                if not ops[e]:
                    continue

                def body(eng_obj, e=e):
                    hw = {}
                    for op in ops[e]:
                        want = {}
                        for d in op.deps:
                            k = d.skey
                            want[k] = max(want.get(k, 0), d.tgt)
                        for k, v in want.items():
                            if v > hw.get(k, 0):
                                eng_obj.wait_ge(sem[k], v)
                                hw[k] = v
                        ins = op.emit(eng_obj)
                        if op.inc:
                            ins.then_inc(sem[op.skey], op.inck)
                    drains = {}
                    for o in ops[e]:
                        if o.isdma and o.inc:
                            drains[o.skey] = cnt[o.skey]
                    for k, v in drains.items():
                        if v > hw.get(k, 0):
                            eng_obj.wait_ge(sem[k], v)

                getattr(blk, _EATTR[e])(body)
        self.ops = None
        self.blk_id += 1


def build_nc():
    nc = bass.Bass("TRN2", target_bir_lowering=False)

    ein = lambda n, s, d: nc.dram_tensor(n, s, d, kind="ExternalInput")
    xb = ein("xb", [SEQ, D], F32)
    winT = ein("winT", [D, DPROJ], F32)
    lnw_r = ein("lnw_r", [D], F32)
    lnb_r = ein("lnb_r", [D], F32)
    dtb_c = ein("dtb_c", [NH, 1], F32)
    alog_c = ein("alog_c", [NH, 1], F32)
    dp_r = ein("dp_r", [1, NH], F32)
    nw_cols = ein("nw_cols", [128, 32], F32)
    woutT = ein("woutT", [DIN, D], F32)
    fc1T = ein("fc1T", [D, INTER], F32)
    fc1b_c = ein("fc1b_c", [128, 64], F32)
    fc2T = ein("fc2T", [INTER, D], F32)
    fc2b_r = ein("fc2b_r", [D], F32)
    wg_i = ein("wg_i", [D, INTER], I8)
    wu_i = ein("wu_i", [D, INTER], I8)
    wd_i = ein("wd_i", [INTER, D], I8)
    sg_c = ein("sg_c", [128, 64], F32)
    su_c = ein("su_c", [128, 64], F32)
    sd_r = ein("sd_r", [D], F32)
    ident_d = ein("ident_d", [128, 128], F32)
    tri_d = ein("tri_d", [128, 128], F32)
    trineg_d = ein("trineg_d", [128, 128], F32)
    ones_d = ein("ones_d", [1, 128], F32)
    rm_c = ein("rm_c", [NH, 1], F32)

    out = nc.dram_tensor("out", [T, D], F32, kind="ExternalOutput")

    d_res = nc.dram_tensor("d_res", [T, D], F32)
    d_h = nc.dram_tensor("d_h", [128, KD * T], BF16)
    d_zsil = nc.dram_tensor("d_zsil", [DIN, T], BF16)
    d_C = nc.dram_tensor("d_C", [DIN, T], BF16)
    d_xf = nc.dram_tensor("d_xf", [DXB, T], BF16)
    d_Bf = nc.dram_tensor("d_Bf", [DXB, T], BF16)
    d_xpf = nc.dram_tensor("d_xpf", [DXB, T], BF16)
    d_Bpf = nc.dram_tensor("d_Bpf", [DXB, T], BF16)
    d_expca = nc.dram_tensor("d_expca", [NH, T], BF16)
    d_yf = nc.dram_tensor("d_yf", [DIN, T], BF16)
    d_u = nc.dram_tensor("d_u", [INTER, T], BF16)
    d_fuse = nc.dram_tensor("d_fuse", [INTER, T], BF16)

    scrmap = {"d_res": (d_res, [T, D]), "d_zsil": (d_zsil, [DIN, T]),
              "d_C": (d_C, [DIN, T]), "d_xf": (d_xf, [DXB, T]),
              "d_Bf": (d_Bf, [DXB, T]), "d_xpf": (d_xpf, [DXB, T]),
              "d_Bpf": (d_Bpf, [DXB, T]), "d_yf": (d_yf, [DIN, T]),
              "d_u": (d_u, [INTER, T]), "d_fuse": (d_fuse, [INTER, T]),
              "d_h": (d_h, [128, KD * T]), "d_expca": (d_expca, [NH, T])}
    taps = []
    for name in _TAPS:
        th, shp = scrmap[name]
        dt = F32 if name == "d_res" else BF16
        taps.append((th, nc.dram_tensor("tap_" + name, shp, dt, kind="ExternalOutput"), shp))

    stG = ExitStack()
    with stG:
        sbG = lambda n, s, d: stG.enter_context(nc.sbuf_tensor(n, s, d))
        ps = lambda n, s, d: stG.enter_context(nc.psum_tensor(n, s, d))

        g_cols = sbG("g_cols", [128, NT], F32)
        ident_f = sbG("ident_f", [128, 128], F32)
        ident_b = sbG("ident_b", [128, 128], BF16)
        tri_f = sbG("tri_f", [128, 128], F32)
        trineg4_b = sbG("trineg4_b", [128, 512], BF16)
        ones_f = sbG("ones_f", [1, 128], F32)
        ones_b = sbG("ones_b", [1, 128], BF16)
        onescol_b = sbG("onescol_b", [128, 1], BF16)
        dtb_sb = sbG("dtb_sb", [NH, 1], F32)
        nega_sb = sbG("nega_sb", [NH, 1], F32)
        dp_sb = sbG("dp_sb", [1, NH], F32)
        dpcols = sbG("dpcols", [NH, NH], F32)
        rm_sb = sbG("rm_sb", [NH, 1], F32)
        cst_eps = sbG("cst_eps", [128, 1], F32)
        cst_one = sbG("cst_one", [128, 1], F32)

        pA = ps("pA", [128, 512], F32)
        pB = ps("pB", [128, 512], F32)
        pC = ps("pC", [128, 512], F32)
        pD = ps("pD", [64, 512], F32)
        pG = ps("pG", [128, 512], F32)
        pT2 = [ps("pT0", [128, 1024], BF16), ps("pT1", [128, 1024], BF16)]
        pF = ps("pF", [128, 512], F32)

        sc = Sched(nc, stG)

        sc.begin()
        sc.add("sp", lambda e: e.dma_start(out=ident_f[:], in_=ident_d.ap()), writes=["ident_f"], dma=True)
        sc.add("sp", lambda e: e.dma_start(out=tri_f[:], in_=tri_d.ap()), writes=["tri_f"], dma=True)
        sc.add("sp", lambda e: e.dma_start(out=ones_f[:], in_=ones_d.ap()), writes=["ones_f"], dma=True)
        sc.add("sp", lambda e: e.dma_start(out=dtb_sb[:], in_=dtb_c.ap()), writes=["dtb"], dma=True)
        sc.add("sp", lambda e: e.dma_start(out=rm_sb[:], in_=rm_c.ap()), writes=["rm"], dma=True)
        sc.add("sp", lambda e: e.dma_start(out=dp_sb[:], in_=dp_r.ap()), writes=["dp"], dma=True)
        sc.add("sp", lambda e: e.dma_start(out=nega_sb[:], in_=alog_c.ap()), writes=["nega"], dma=True)
        sc.add("pool", lambda e: e.dma_start(out=ident_b[:], in_=ident_d.ap()), writes=["ident_b"], dma=True)
        sc.add("pool", lambda e: e.dma_start(out=ones_b[:], in_=ones_d.ap()), writes=["ones_b"], dma=True)
        sc.add("pool", lambda e: e.dma_start(out=trineg4_b[:], in_=dap(trineg_d, 0, [[128, 128], [0, 4], [1, 128]])), writes=["trineg"], dma=True)
        for _q in range(4):
            sc.add("pool", (lambda e, _q=_q: e.dma_start(
                out=d_res[_q * 512:(_q + 1) * 512, :],
                in_=dap(fc2b_r, 0, [[0, 512], [1, D]]))), writes=["d_res"], dma=True)
        sc.add("dve", lambda e: e.memset(onescol_b[:], 1.0), writes=["onescol"])
        sc.add("dve", lambda e: e.memset(cst_eps[:], EPS), writes=["cst_eps"])
        sc.add("dve", lambda e: e.memset(cst_one[:], 1.0), writes=["cst_one"])
        nc.const_aps.register(F32, EPS, cst_eps[:])
        nc.const_aps.register(F32, 1.0, cst_one[:])
        sc.add("act", lambda e: e.activation(nega_sb[:], nega_sb[:], AF.Exp), reads=["nega"], writes=["nega"])
        sc.add("pe", lambda e: e.matmul(pG[0:NH, 0:NH], ones_f[0:1, 0:NH], dp_sb[:], start=True, stop=True),
               reads=["ones_f", "dp"], writes=["pG"])
        sc.add("dve", lambda e: e.tensor_copy(dpcols[:], pG[0:NH, 0:NH]), reads=["pG"], writes=["dpcols"])
        sc.end()

        def emit_hpass(tag, hbuf, src_dram, src_base, *, affine, res_add):
            with ExitStack() as stP:
                sb = lambda n, s, d: stP.enter_context(nc.sbuf_tensor(n, s, d))
                xt = [sb(f"{tag}_xt{i}", [128, D], F32) for i in range(2)]
                xm = sb(f"{tag}_xm", [128, D], F32)
                sq = sb(f"{tag}_sq", [128, D], F32)
                h0 = sb(f"{tag}_h0", [128, D], F32)
                rt = sb(f"{tag}_rt", [128, D], F32)
                hbf = [sb(f"{tag}_hbf{i}", [128, D], BF16) for i in range(2)]
                cols = sb(f"{tag}_cols", [128, 8], F32)
                junk = sb(f"{tag}_junk", [128, 1], F32)
                if affine:
                    lnwb = sb(f"{tag}_lnw", [128, D], F32)
                    lnbb = sb(f"{tag}_lnb", [128, D], F32)

                sc.begin(tag)
                if affine:
                    sc.add("sp", lambda e: e.dma_start(out=lnwb[:], in_=dap(lnw_r, 0, [[0, 128], [1, D]])),
                           writes=["lnwb"], dma=True)
                    sc.add("sp", lambda e: e.dma_start(out=lnbb[:], in_=dap(lnb_r, 0, [[0, 128], [1, D]])),
                           writes=["lnbb"], dma=True)
                for i in range(NT):
                    xti = xt[i % 2]
                    Rx = f"xt{i%2}"
                    sc.add("sp", (lambda e, i=i, xti=xti: e.dma_start(
                        out=xti[:], in_=src_dram[src_base + i * 128: src_base + (i + 1) * 128, :])),
                        writes=[Rx], dma=True)
                    sc.add("act", (lambda e, xti=xti: e.activation(sq[:], xti[:], AF.Copy, accum_out=cols[:, 0:1])),
                           reads=[Rx], writes=["sq", "c0"])
                    sc.add("dve", lambda e: e.tensor_scalar_mul(cols[:, 0:1], cols[:, 0:1], 1.0 / D),
                           reads=["c0"], writes=["c0"])
                    sc.add("dve", (lambda e, xti=xti: e.tensor_scalar(xm[:], xti[:], cols[:, 0:1], None, op0=ALU.subtract)),
                           reads=[Rx, "c0"], writes=["xm"])
                    sc.add("act", lambda e: e.activation(sq[:], xm[:], AF.Square, accum_out=cols[:, 1:2]),
                           reads=["xm"], writes=["sq", "c1"])
                    sc.add("act", lambda e: e.activation(cols[:, 2:3], cols[:, 1:2], AF.Ln, bias=EPS, scale=1.0 / D),
                           reads=["c1"], writes=["c2"])
                    sc.add("act", lambda e: e.activation(cols[:, 3:4], cols[:, 2:3], AF.Exp, scale=-0.5),
                           reads=["c2"], writes=["c3"])
                    sc.add("dve", lambda e: e.tensor_scalar(h0[:], xm[:], cols[:, 3:4], None, op0=ALU.mult),
                           reads=["xm", "c3"], writes=["h0"])
                    if res_add:
                        sc.add("dve", (lambda e, xti=xti: e.tensor_add(rt[:], xti[:], h0[:])),
                               reads=[Rx, "h0"], writes=["rt"])
                        sc.add("pool", (lambda e, i=i: e.dma_start(
                            out=d_res[i * 128:(i + 1) * 128, :], in_=rt[:], accum_op=ALU.add)),
                            reads=["rt"], dma=True)
                    if affine:
                        sc.add("act", lambda e: e.activation(sq[:], h0[:], AF.Copy, accum_out=cols[:, 4:5]),
                               reads=["h0"], writes=["sq", "c4"])
                        sc.add("dve", lambda e: e.tensor_scalar_mul(cols[:, 4:5], cols[:, 4:5], 1.0 / D),
                               reads=["c4"], writes=["c4"])
                        sc.add("dve", lambda e: e.tensor_scalar(xm[:], h0[:], cols[:, 4:5], None, op0=ALU.subtract),
                               reads=["h0", "c4"], writes=["xm"])
                        sc.add("act", lambda e: e.activation(sq[:], xm[:], AF.Square, accum_out=cols[:, 5:6]),
                               reads=["xm"], writes=["sq", "c5"])
                        sc.add("act", lambda e: e.activation(cols[:, 6:7], cols[:, 5:6], AF.Ln, bias=EPS, scale=1.0 / D),
                               reads=["c5"], writes=["c6"])
                        sc.add("act", lambda e: e.activation(cols[:, 7:8], cols[:, 6:7], AF.Exp, scale=-0.5),
                               reads=["c6"], writes=["c7"])
                        sc.add("dve", lambda e: e.tensor_scalar(h0[:], xm[:], cols[:, 7:8], None, op0=ALU.mult),
                               reads=["xm", "c7"], writes=["h0"])
                        sc.add("dve", lambda e: e.tensor_mul(h0[:], h0[:], lnwb[:]),
                               reads=["h0", "lnwb"], writes=["h0"])
                        hb = hbf[i % 2]
                        sc.add("dve", (lambda e, hb=hb: e.tensor_add(hb[:], h0[:], lnbb[:])),
                               reads=["h0", "lnbb"], writes=[f"hbf{i%2}"])
                    else:
                        hb = hbf[i % 2]
                        sc.add("dve", (lambda e, hb=hb: e.tensor_scalar(hb[:], xm[:], cols[:, 3:4], None, op0=ALU.mult)),
                               reads=["xm", "c3"], writes=[f"hbf{i%2}"])
                    for k in range(KD):
                        hb = hbf[i % 2]
                        slot = k % 2
                        sc.add("pe", (lambda e, hb=hb, k=k, slot=slot: e.transpose(
                            pT2[slot][:, 0:128], hb[:, k * 128:(k + 1) * 128], ident_b[:])),
                            reads=[f"hbf{i%2}", "ident_b"], writes=[f"pT{slot}"])
                        sc.add("act", (lambda e, i=i, k=k, slot=slot: e.copy(
                            hbuf[:, k * T + i * 128: k * T + (i + 1) * 128],
                            pT2[slot][:, 0:128])),
                            reads=[f"pT{slot}"], writes=["hbuf"])
                sc.end()

        def emit_inproj(tag, hbuf, otiles, dsts, stf):
            with ExitStack() as stP:
                sb = lambda n, s, d: stP.enter_context(nc.sbuf_tensor(n, s, d))
                wb = [sb(f"{tag}_wb{i}", [128, KD * 128], BF16) for i in range(2)]
                stg = [sb(f"{tag}_st{i}", [128, 512], BF16) for i in range(4)]
                sc.begin(tag)
                for j, ot in enumerate(otiles):
                    wbj = wb[j % 2]
                    Rw = f"wb{j%2}"
                    wcols = 64 if ot == OT_DT else 128
                    for half in range(2):
                        sc.add("pool", (lambda e, wbj=wbj, ot=ot, half=half, wcols=wcols: e.dma_start(
                            out=wbj[:, half * 1024:(half + 1) * 1024].rearrange(
                                "p (k o) -> p k o", k=8)[:, :, 0:wcols],
                            in_=dap(winT, ot * 128 + half * 8 * 128 * DPROJ,
                                    [[DPROJ, 128], [128 * DPROJ, 8], [1, wcols]]))),
                            writes=[Rw], dma=True)
                    m = 64 if ot == OT_DT else 128
                    for tc in range(4):
                        p = pA if tc % 2 == 0 else pB
                        Rp = "pA" if tc % 2 == 0 else "pB"

                        def mmg(e, wbj=wbj, p=p, m=m, tc=tc):
                            for k in range(KD):
                                ins = e.matmul(p[0:m, :], wbj[0:128, k * 128: k * 128 + m],
                                               hbuf[:, k * T + tc * 512: k * T + (tc + 1) * 512],
                                               start=(k == 0), stop=(k == KD - 1))
                            return ins
                        sc.add("pe", mmg, reads=[Rw, "hbuf"], writes=[Rp])
                        idx = j * 4 + tc
                        sg = stg[idx % 4]
                        Rs = f"stg{idx%4}"
                        if ot == OT_DT:
                            sc.add("act", (lambda e, p=p, tc=tc: e.copy(
                                stf[0:64, tc * 512:(tc + 1) * 512], p[0:64, :])),
                                reads=[Rp], writes=["stf"])
                            continue
                        if ot < OT_X:
                            sc.add("act", (lambda e, sg=sg, p=p: e.activation(sg[:], p[:], AF.Silu)),
                                   reads=[Rp], writes=[Rs])
                            dst, row = dsts[0], ot * 128
                        else:
                            sc.add("act", (lambda e, sg=sg, p=p: e.copy(sg[:], p[:])),
                                   reads=[Rp], writes=[Rs])
                            if ot < OT_B:
                                dst, row = dsts[1], (ot - OT_X) * 128
                            elif ot < OT_C:
                                dst, row = dsts[2], (ot - OT_B) * 128
                            else:
                                dst, row = dsts[3], (ot - OT_C) * 128
                        sc.add("sp", (lambda e, dst=dst, row=row, tc=tc, sg=sg: e.dma_start(
                            out=dst[row:row + 128, tc * 512:(tc + 1) * 512], in_=sg[:])),
                            reads=[Rs], dma=True)
                sc.end()

        def emit_dtprep(tag, stf, a_dst, ldt_dst):
            with ExitStack() as stP:
                sb = lambda n, s, d: stP.enter_context(nc.sbuf_tensor(n, s, d))
                t0 = sb(f"{tag}_t0", [NH, T], F32)
                t1 = sb(f"{tag}_t1", [NH, T], F32)
                sc.begin()
                sc.add("dve", lambda e: e.tensor_scalar(t0[:], stf[:], dtb_sb[:], None, op0=ALU.add),
                       reads=["stf", "dtb"], writes=["t0"])
                sc.add("act", lambda e: e.activation(t1[:], t0[:], AF.Abs), reads=["t0"], writes=["t1"])
                sc.add("act", lambda e: e.activation(t1[:], t1[:], AF.Exp, scale=-1.0), reads=["t1"], writes=["t1"])
                sc.add("act", lambda e: e.activation(t1[:], t1[:], AF.Ln, bias=1.0), reads=["t1"], writes=["t1"])
                sc.add("dve", lambda e: e.tensor_scalar_max(stf[:], t0[:], 0.0), reads=["t0"], writes=["stf"])
                sc.add("dve", lambda e: e.tensor_add(stf[:], stf[:], t1[:]), reads=["stf", "t1"], writes=["stf"])
                sc.add("dve", (lambda e: e.tensor_scalar(a_dst[:], stf[:], nega_sb[:], -1.0,
                                                         op0=ALU.mult, op1=ALU.mult)),
                       reads=["stf", "nega"], writes=["a_dst"])
                sc.add("act", lambda e: e.activation(ldt_dst[:], stf[:], AF.Ln), reads=["stf"], writes=["ldt_dst"])
                sc.end()

        def emit_scan(tag, full, a_sb, ldt_sb, src_x, src_B, state):
            with ExitStack() as stP:
                sb = lambda n, s, d: stP.enter_context(nc.sbuf_tensor(n, s, d))
                xcf2 = [sb(f"{tag}_xcf{pb}", [128, 8 * 256], BF16) for pb in range(2)]
                Bcf2 = [[sb(f"{tag}_Bcf{p}{pb}", [64, 8 * 256], BF16) for pb in range(2)]
                        for p in range(2)]
                pbof = lambda c: (c // 2) % 2
                xcf_sl = lambda c, k: xcf2[pbof(c)][:, (2 * k + c % 2) * 128:(2 * k + c % 2 + 1) * 128]
                Bcf_sl = lambda par, c, k: Bcf2[par][pbof(c)][0:64, (2 * k + c % 2) * 128:(2 * k + c % 2 + 1) * 128]
                xct = sb(f"{tag}_xct", [128, 8 * 128], BF16)
                Bct = sb(f"{tag}_Bct", [128, 8 * 128], BF16)
                aT = sb(f"{tag}_aT", [128, 64], F32)
                ldtT = sb(f"{tag}_ldtT", [128, 64], F32)
                caT = sb(f"{tag}_caT", [128, 64], F32)
                colB = sb(f"{tag}_colB", [128, 64], F32)
                calast = sb(f"{tag}_cal", [1, 64], F32)
                dec_b = sb(f"{tag}_dec", [1, 64], BF16)
                dec_cols = sb(f"{tag}_decc", [NH, NH], F32)
                wall = sb(f"{tag}_wall", [128, 64], F32)
                btil = [sb(f"{tag}_btl{i}", [128, 256], BF16) for i in range(2)]
                if full:
                    ca_fm = sb(f"{tag}_cafm", [64, 128], F32)
                    Cc2 = [[sb(f"{tag}_Cc{p}{pb}", [64, 32 * 256], BF16) for pb in range(2)]
                           for p in range(2)]
                    zc2 = [sb(f"{tag}_zc0", [128, 32 * 256], BF16)]
                    Cc_sl = lambda par, c, k: Cc2[par][pbof(c)][0:64, (2 * k + c % 2) * 128:(2 * k + c % 2 + 1) * 128]
                    Cc3 = lambda par, c: Cc2[par][pbof(c)][:].rearrange("p (k s t) -> p k s t", k=32, s=2)[0:64, :, c % 2, :]
                    zc3 = lambda c: zc2[0][:].rearrange("p (k s t) -> p k s t", k=32, s=2)[:, :, c % 2, :]
                    cbc = [sb(f"{tag}_cbc{p}", [64, 32 * 128], BF16) for p in range(2)]
                    chat = cbc
                    e4 = sb(f"{tag}_e4", [128, 512], BF16)
                    p4 = [sb(f"{tag}_p4{i}", [128, 512], BF16) for i in range(2)]
                    dpx = sb(f"{tag}_dpx", [128, 32 * 128], BF16)
                    ypre = sb(f"{tag}_ypre", [128, 32 * 128], BF16)
                    yf = sb(f"{tag}_yf", [128, 32 * 128], BF16)
                    cam = sb(f"{tag}_cam", [1, 1024], F32)
                    expca = sb(f"{tag}_expca", [64, 128], F32)
                    grow = sb(f"{tag}_grow", [1, 128], F32)
                    st_bf = sb(f"{tag}_stbf", [NH, NH * 64], BF16)
                    nw_sb = sb(f"{tag}_nw", [128, 32], F32)

                def loads(c):
                    pb = pbof(c)
                    for half in range(2):
                        sc.add("sp", (lambda e, c=c, pb=pb, half=half: e.dma_start(
                            out=xcf2[pb][:, half * 1024:(half + 1) * 1024].rearrange("p (k t) -> p k t", k=4),
                            in_=dap(src_x, c * 128 + half * 4 * 128 * T,
                                    [[T, 128], [128 * T, 4], [1, 256]]))),
                            writes=[f"xcf{pb}"], dma=True)
                    for par in range(2):
                        sc.add("sp", (lambda e, c=c, pb=pb, par=par: e.dma_start(
                            out=Bcf2[par][pb][:].rearrange("p (k t) -> p k t", k=8),
                            in_=dap(src_B, c * 128 + par * 64 * T,
                                    [[T, 64], [128 * T, 8], [1, 256]]))),
                            writes=[f"Bcf{par}{pb}"], dma=True)
                    if full:
                        for par in range(2):
                            for hf in range(2):
                                sc.add("sp", (lambda e, c=c, pb=pb, par=par, hf=hf: e.dma_start(
                                    out=Cc2[par][pb][:, hf * 4096:(hf + 1) * 4096].rearrange("p (k t) -> p k t", k=16),
                                    in_=dap(d_C, c * 128 + par * 64 * T + hf * 16 * 128 * T,
                                            [[T, 64], [128 * T, 16], [1, 256]]))),
                                    writes=[f"Cc{par}{pb}"], dma=True)


                sc.begin(tag + '_pre')
                if full:
                    sc.add("sp", lambda e: e.dma_start(out=nw_sb[:], in_=nw_cols.ap()),
                           writes=["nw"], dma=True)
                loads(0)
                sc.end()


                for c in range(NT):
                    cb = c % 2
                    sc.begin(f'{tag}_c{c}')
                    if full and c % 2 == 0:
                        for qr in range(4):
                            sc.add("sp", (lambda e, c=c, qr=qr: e.dma_start(
                                out=zc2[0][:, qr * 2048:(qr + 1) * 2048].rearrange("p (k t) -> p k t", k=8),
                                in_=dap(d_zsil, c * 128 + qr * 8 * 128 * T,
                                        [[T, 128], [128 * T, 8], [1, 256]]))),
                                writes=["zc0"], dma=True)
                    if c % 2 == 0 and c + 2 < NT:
                        loads(c + 2)
                    for k in range(8):
                        slot = k % 2
                        sc.add("pe", (lambda e, cb=cb, k=k, slot=slot: e.transpose(
                            pT2[slot][:, 0:128], xcf_sl(c, k), ident_b[:])),
                            reads=[f"xcf{pbof(c)}", "ident_b"], writes=[f"pT{slot}"])
                        sc.add("dve", (lambda e, k=k, slot=slot: e.tensor_copy(
                            xct[:, k * 128:(k + 1) * 128], pT2[slot][:, 0:128])),
                            reads=[f"pT{slot}"], writes=["xct"])
                    for g in range(16):
                        par, jj = g % 2, g // 2
                        slot = g % 2
                        sc.add("pe", (lambda e, cb=cb, par=par, jj=jj, slot=slot: e.transpose(
                            pT2[slot][0:128, 0:64],
                            Bcf_sl(par, c, jj), ident_b[0:64, 0:64])),
                            reads=[f"Bcf{par}{pbof(c)}", "ident_b"], writes=[f"pT{slot}"])
                        sc.add("dve", (lambda e, g=g, slot=slot: e.tensor_copy(
                            Bct[:, g * 64:(g + 1) * 64], pT2[slot][0:128, 0:64])),
                            reads=[f"pT{slot}"], writes=["Bct"])
                    sc.add("pe", (lambda e, c=c: e.transpose(pF[0:128, 0:64],
                                                             a_sb[0:64, c * 128:(c + 1) * 128],
                                                             ident_f[0:64, 0:64])),
                           reads=["a_sb", "ident_f"], writes=["pF"])
                    sc.add("dve", lambda e: e.tensor_copy(aT[:, 0:64], pF[0:128, 0:64]),
                           reads=["pF"], writes=["aT"])
                    sc.add("pe", (lambda e, c=c: e.transpose(pF[0:128, 0:64],
                                                             ldt_sb[0:64, c * 128:(c + 1) * 128],
                                                             ident_f[0:64, 0:64])),
                           reads=["ldt_sb", "ident_f", "aT"], writes=["pF"])
                    sc.add("dve", lambda e: e.tensor_copy(ldtT[:, 0:64], pF[0:128, 0:64]),
                           reads=["pF"], writes=["ldtT"])
                    sc.add("pe", lambda e: e.matmul(pG[0:128, 0:64], tri_f[:], aT[:, 0:64], start=True, stop=True),
                           reads=["tri_f", "aT"], writes=["pG"])
                    sc.add("dve", lambda e: e.tensor_copy(caT[:, 0:64], pG[0:128, 0:64]),
                           reads=["pG"], writes=["caT"])
                    sc.add("dve", lambda e: e.tensor_sub(colB[:, 0:64], ldtT[:, 0:64], caT[:, 0:64]),
                           reads=["ldtT", "caT"], writes=["colB"])
                    sc.add("act", lambda e: e.dma_start(out=calast[:], in_=caT[127:128, 0:64]),
                           reads=["caT"], writes=["calast"], dma=True)
                    sc.add("act", lambda e: e.activation(dec_b[:], calast[:], AF.Exp),
                           reads=["calast"], writes=["dec_b"])

                    def wallmm(e):
                        e.matmul(pG[0:128, 64:128], ones_f[:], calast[:], start=True, stop=False)
                        return e.matmul(pG[0:128, 64:128], ident_f[:], colB[:, 0:64], start=False, stop=True)
                    sc.add("pe", wallmm, reads=["ones_f", "calast", "ident_f", "colB"], writes=["pG"])
                    sc.add("act", lambda e: e.activation(wall[:, 0:64], pG[0:128, 64:128], AF.Exp),
                           reads=["pG"], writes=["wall"])
                    sc.add("pe", lambda e: e.matmul(pG[0:NH, 128:192], ones_b[0:1, 0:NH], dec_b[:], start=True, stop=True),
                           reads=["ones_b", "dec_b"], writes=["pG"])
                    sc.add("dve", lambda e: e.tensor_copy(dec_cols[:, 0:NH], pG[0:NH, 128:192]),
                           reads=["pG"], writes=["dec_cols"])

                    if full:
                        sc.add("pe", (lambda e: e.transpose(pF[0:64, 0:128], caT[0:128, 0:64], ident_f[:])),
                               reads=["caT", "ident_f", "ldtT"], writes=["pF"])
                        sc.add("dve", lambda e: e.tensor_copy(ca_fm[:], pF[0:64, 0:128]),
                               reads=["pF"], writes=["ca_fm"])
                        sc.add("act", lambda e: e.activation(expca[:], ca_fm[:], AF.Exp),
                               reads=["ca_fm"], writes=["expca"])
                        sc.add("pool", (lambda e, c=c: e.dma_start(
                            out=d_expca[0:64, c * 128:(c + 1) * 128], in_=expca[:])),
                            reads=["expca"], writes=["d_expca"], dma=True)
                        for par in range(2):
                            sc.add("pool", (lambda e, c=c, par=par: e.dma_start(
                                out=cbc[par][:], in_=dap(d_expca, par * T + c * 128,
                                                         [[0, 64], [2 * T, 32], [1, 128]]))),
                                reads=["d_expca"], writes=[f"cbc{par}"], dma=True)
                            sc.add("dve", (lambda e, c=c, par=par: e.tensor_tensor(
                                chat[par][:].rearrange("p (k t) -> p k t", k=32),
                                Cc3(par, c),
                                cbc[par][:].rearrange("p (k t) -> p k t", k=32), op=ALU.mult)),
                                reads=[f"Cc{par}{pbof(c)}", f"cbc{par}"], writes=[f"chat{par}", f"cbc{par}"])
                        sc.add("dve", lambda e: e.tensor_copy(st_bf[:], state[:]),
                               reads=["state"], writes=["st_bf"])

                    for q in range(16):
                        hs = [4 * q + i for i in range(4)]
                        g = q
                        gb, gt = 64 * (g % 2), (g // 2) * 128
                        if full:
                            if q % 2 == 0:
                                aa = q // 2
                                sc.add("pool", (lambda e, aa=aa: e.dma_start(
                                    out=cam[:], in_=ca_fm[aa * 8:(aa + 1) * 8, :])),
                                    reads=["ca_fm"], writes=["cam"], dma=True)

                            def d2mm(e, q=q):
                                e.matmul(pA[:], ones_f[:], cam[0:1, (q % 2) * 512:(q % 2 + 1) * 512],
                                         start=True, stop=False)
                                return e.matmul(pA[:], ident_b[:], trineg4_b[:], start=False, stop=True)
                            sc.add("pe", d2mm, reads=["cam", "ones_f", "ident_b", "trineg"], writes=["pA"])

                            def spmm(e, hs=hs, g=g, c=c):
                                for i, h in enumerate(hs):
                                    ins = e.matmul(pB[:, i * 128:(i + 1) * 128],
                                                   Bcf_sl(g % 2, c, g // 2),
                                                   Cc_sl(h % 2, c, h // 2),
                                                   start=True, stop=True)
                                return ins
                            sc.add("pe", spmm,
                                   reads=[f"Bcf{g%2}{pbof(c)}", f"Cc0{pbof(c)}", f"Cc1{pbof(c)}"], writes=["pB"])
                            for i, h in enumerate(hs):
                                sc.add("act", (lambda e, i=i, h=h: e.activation(
                                    e4[:, i * 128:(i + 1) * 128], pA[:, i * 128:(i + 1) * 128],
                                    AF.Exp, bias=colB[:, h:h + 1])),
                                    reads=["pA", "colB"], writes=["e4"])
                            p4q = p4[q % 2]
                            sc.add("dve", (lambda e, p4q=p4q: e.tensor_mul(p4q[:], pB[:], e4[:])),
                                   reads=["pB", "e4"], writes=[f"p4_{q%2}"])

                            def ymm(e, hs=hs, g=g, p4q=p4q):
                                for i, h in enumerate(hs):
                                    bp = 64 * (h % 2)
                                    col = ((h % 8) // 2) * 128
                                    e.matmul(pC[bp:bp + 64, col:col + 128],
                                             xct[:, g * 64:g * 64 + 64],
                                             p4q[:, i * 128:(i + 1) * 128],
                                             start=True, stop=False, tile_position=(0, bp))
                                    ins = e.matmul(pC[bp:bp + 64, col:col + 128],
                                                   st_bf[0:64, h * 64:(h + 1) * 64],
                                                   chat[h % 2][0:64,
                                                               (h // 2) * 128:(h // 2) * 128 + 128],
                                                   start=False, stop=True, tile_position=(0, bp))
                                return ins
                            sc.add("pe", ymm, reads=[f"p4_{q%2}", "xct", "st_bf", "chat0", "chat1"],
                                   writes=["pC"])
                            if q % 2 == 1:
                                def dpxb(e, q=q, c=c):
                                    for h in range(4 * q - 4, 4 * q + 4):
                                        gg = h // 4
                                        bp = 64 * (h % 2)
                                        col = (h // 2) * 128
                                        xb_ = 64 * (gg % 2)
                                        ins = e.tensor_scalar(dpx[bp:bp + 64, col:col + 128],
                                                              xcf_sl(c, gg // 2)[xb_:xb_ + 64, :],
                                                              dpcols[:, h:h + 1], None, op0=ALU.mult)
                                    return ins
                                sc.add("dve", dpxb, reads=[f"xcf{pbof(c)}", "dpcols"], writes=["dpx"])
                                a = q // 2
                                sc.add("dve", (lambda e, a=a: e.tensor_add(
                                    ypre[:, a * 512:(a + 1) * 512], dpx[:, a * 512:(a + 1) * 512], pC[:])),
                                    reads=["dpx", "pC"], writes=["ypre"])
                        btq = btil[q % 2]

                        def btb(e, hs=hs, g=g, btq=btq):
                            for i, h in enumerate(hs):
                                ins = e.tensor_scalar(btq[:, i * 64:(i + 1) * 64],
                                                      Bct[:, g * 64:g * 64 + 64],
                                                      wall[:, h:h + 1], None, op0=ALU.mult)
                            return ins
                        sc.add("dve", btb, reads=["Bct", "wall"], writes=[f"btil{q%2}"])

                        def saddmm(e, hs=hs, g=g, btq=btq):
                            for i, h in enumerate(hs):
                                ins = e.matmul(pD[0:64, (h % 8) * 64:(h % 8) * 64 + 64],
                                               btq[:, i * 64:(i + 1) * 64],
                                               xct[:, g * 64:g * 64 + 64],
                                               start=True, stop=True)
                            return ins
                        sc.add("pe", saddmm, reads=[f"btil{q%2}", "xct"], writes=["pD"])
                        if q % 2 == 1:
                            def stt(e, q=q):
                                for h in range(4 * q - 4, 4 * q + 4):
                                    ins = e.scalar_tensor_tensor(
                                        state[0:64, h * 64:(h + 1) * 64],
                                        state[0:64, h * 64:(h + 1) * 64],
                                        dec_cols[:, h:h + 1],
                                        pD[0:64, (h % 8) * 64:(h % 8) * 64 + 64],
                                        op0=ALU.mult, op1=ALU.add)
                                return ins
                            sc.add("dve", stt, reads=["pD", "dec_cols", "state"], writes=["state"])

                    if full:
                        sc.add("dve", (lambda e, c=c: e.tensor_tensor(
                            yf[:].rearrange("p (k t) -> p k t", k=32),
                            ypre[:].rearrange("p (k t) -> p k t", k=32),
                            zc3(c), op=ALU.mult)),
                            reads=["ypre", "zc0"], writes=["yf"])
                        sc.add("dve", lambda e: e.tensor_mul(dpx[:], yf[:], yf[:]),
                               reads=["yf"], writes=["dpx"])
                        nwrep = bass.AP(nw_sb.tensor if hasattr(nw_sb, "tensor") else nw_sb,
                                        0, [[32, 128], [1, 32], [0, 128]])
                        sc.add("dve", (lambda e, nwrep=nwrep: e.tensor_tensor(
                            ypre[:].rearrange("p (j t) -> p j t", j=32), yf[:].rearrange("p (j t) -> p j t", j=32),
                            nwrep, op=ALU.mult)),
                            reads=["yf", "nw"], writes=["ypre"])

                        def ssmm(e):
                            for k in range(32):
                                ins = e.matmul(pG[0:1, 192:320], onescol_b[:],
                                               dpx[:, k * 128:(k + 1) * 128],
                                               start=(k == 0), stop=(k == 31))
                            return ins
                        sc.add("pe", ssmm, reads=["dpx", "onescol"], writes=["pG"])
                        sc.add("act", lambda e: e.activation(grow[:], pG[0:1, 192:320], AF.Ln,
                                                             scale=1.0 / DIN, bias=EPS),
                               reads=["pG"], writes=["grow"])
                        sc.add("act", lambda e: e.activation(grow[:], grow[:], AF.Exp, scale=-0.5),
                               reads=["grow"], writes=["grow"])
                        sc.add("pe", lambda e: e.matmul(pG[0:128, 320:321], grow[:], ones_f[0:1, 0:1],
                                                        start=True, stop=True),
                               reads=["grow", "ones_f"], writes=["pG"])
                        sc.add("dve", (lambda e, c=c: e.tensor_copy(g_cols[:, c:c + 1], pG[0:128, 320:321])),
                               reads=["pG"], writes=["g_cols"])
                        for qr in range(4):
                            sc.add("pool", (lambda e, c=c, qr=qr: e.dma_start(
                                out=dap(d_yf, c * 128 + qr * 8 * 128 * T, [[T, 128], [128 * T, 8], [1, 128]]),
                                in_=ypre[:, qr * 1024:(qr + 1) * 1024])),
                                reads=["ypre"], writes=["d_yf"], dma=True)
                    sc.end()

        def emit_mlp(hbuf):
            with ExitStack() as stP:
                sb = lambda n, s, d: stP.enter_context(nc.sbuf_tensor(n, s, d))
                wb = [sb(f"m_wb{i}", [128, KD * 128], BF16) for i in range(2)]
                t0 = [sb(f"m_t0{i}", [128, 512], F32) for i in range(2)]
                t1 = [sb(f"m_t1{i}", [128, 512], F32) for i in range(2)]
                stg = [sb(f"m_st{i}", [128, 512], BF16) for i in range(4)]
                fb = sb("m_fb", [128, 64], F32)
                sc.begin('mlp')
                sc.add("sp", lambda e: e.dma_start(out=fb[:], in_=fc1b_c.ap()), writes=["fb"], dma=True)
                for j in range(64):
                    wbj = wb[j % 2]
                    Rw = f"mwb{j%2}"
                    for half in range(2):
                        sc.add("pool", (lambda e, wbj=wbj, j=j, half=half: e.dma_start(
                            out=wbj[:, half * 1024:(half + 1) * 1024],
                            in_=dap(fc1T, j * 128 + half * 8 * 128 * INTER,
                                    [[INTER, 128], [128 * INTER, 8], [1, 128]]))),
                            writes=[Rw], dma=True)
                    for tc in range(4):
                        p = pA if tc % 2 == 0 else pB
                        Rp = "pA" if tc % 2 == 0 else "pB"

                        def mmg(e, wbj=wbj, p=p, tc=tc):
                            for k in range(KD):
                                ins = e.matmul(p[:], wbj[:, k * 128:(k + 1) * 128],
                                               hbuf[:, k * T + tc * 512: k * T + (tc + 1) * 512],
                                               start=(k == 0), stop=(k == KD - 1))
                            return ins
                        sc.add("pe", mmg, reads=[Rw, "hbuf"], writes=[Rp])
                        idx = j * 4 + tc
                        tt0, tt1, sg = t0[idx % 2], t1[idx % 2], stg[idx % 4]
                        R0, R1, Rs = f"mt0{idx%2}", f"mt1{idx%2}", f"mst{idx%4}"
                        sc.add("act", (lambda e, tt0=tt0, p=p, j=j: e.activation(
                            tt0[:], p[:], AF.Identity, bias=fb[:, j:j + 1])),
                            reads=[Rp, "fb"], writes=[R0])
                        sc.add("dve", (lambda e, tt0=tt0, tt1=tt1: e.tensor_mul(tt1[:], tt0[:], tt0[:])),
                               reads=[R0], writes=[R1])
                        sc.add("dve", (lambda e, tt1=tt1: e.tensor_scalar(tt1[:], tt1[:], 0.044715, 1.0,
                                                                          op0=ALU.mult, op1=ALU.add)),
                               reads=[R1], writes=[R1])
                        sc.add("dve", (lambda e, tt0=tt0, tt1=tt1: e.tensor_mul(tt1[:], tt1[:], tt0[:])),
                               reads=[R0, R1], writes=[R1])
                        sc.add("act", (lambda e, tt1=tt1: e.activation(tt1[:], tt1[:], AF.Sigmoid, scale=GELU_K)),
                               reads=[R1], writes=[R1])
                        sc.add("dve", (lambda e, sg=sg, tt0=tt0, tt1=tt1: e.tensor_mul(sg[:], tt1[:], tt0[:])),
                               reads=[R0, R1], writes=[Rs])
                        sc.add("sp", (lambda e, sg=sg, j=j, tc=tc: e.dma_start(
                            out=d_u[j * 128:(j + 1) * 128, tc * 512:(tc + 1) * 512], in_=sg[:])),
                            reads=[Rs], dma=True)
                sc.end()

        def emit_p23b():
            with ExitStack() as stP:
                sb = lambda n, s, d: stP.enter_context(nc.sbuf_tensor(n, s, d))
                wqb = sb("o_wqb", [128, 32 * 512], BF16)
                f2b = sb("o_f2b", [128, 64 * 512], BF16)
                yfc = [sb(f"o_yf{i}", [128, 32 * 256], BF16) for i in range(2)]
                uc = [sb(f"o_uc{i}", [128, 64 * 256], BF16) for i in range(2)]
                rsb = [sb(f"o_rs{i}", [128, 2 * 512], F32) for i in range(2)]
                ost = [sb(f"o_ost{i}", [128, 512], F32) for i in range(2)]
                for qq in range(4):
                    sc.begin(f'p23b_q{qq}')
                    for gset in range(4):
                        sc.add("pool", (lambda e, qq=qq, gset=gset: e.dma_start(
                            out=wqb[:, gset * 4096:(gset + 1) * 4096],
                            in_=dap(woutT, qq * 512 + gset * 8 * 128 * D,
                                    [[D, 128], [128 * D, 8], [1, 512]]))),
                            writes=["wqb"], dma=True)
                    for gset in range(8):
                        sc.add("pool", (lambda e, qq=qq, gset=gset: e.dma_start(
                            out=f2b[:, gset * 4096:(gset + 1) * 4096],
                            in_=dap(fc2T, qq * 512 + gset * 8 * 128 * D,
                                    [[D, 128], [128 * D, 8], [1, 512]]))),
                            writes=["f2b"], dma=True)
                    for i2 in range(8):
                        ib = i2 % 2
                        for gset in range(4):
                            sc.add("sp", (lambda e, i2=i2, ib=ib, gset=gset: e.dma_start(
                                out=yfc[ib][:, gset * 2048:(gset + 1) * 2048],
                                in_=dap(d_yf, i2 * 256 + gset * 8 * 128 * T,
                                        [[T, 128], [128 * T, 8], [1, 256]]))),
                                writes=[f"yfc{ib}"], dma=True)
                        for gset in range(8):
                            qeng = "act" if gset % 2 == 0 else "sp"
                            sc.add(qeng, (lambda e, i2=i2, ib=ib, gset=gset: e.dma_start(
                                out=uc[ib][:, gset * 2048:(gset + 1) * 2048],
                                in_=dap(d_u, i2 * 256 + gset * 8 * 128 * T,
                                        [[T, 128], [128 * T, 8], [1, 256]]))),
                                writes=[f"uc{ib}"], dma=True)
                        sc.add("sp", (lambda e, i2=i2, ib=ib, qq=qq: e.dma_start(
                            out=rsb[ib][:].rearrange("p (s n) -> p s n", s=2),
                            in_=dap(d_res, (i2 * 256) * D + qq * 512,
                                    [[D, 128], [128 * D, 2], [1, 512]]))),
                            reads=[f"dres{i2}"], writes=[f"rsb{ib}"], dma=True)
                        for sub in range(2):
                            i = i2 * 2 + sub
                            io = i % 2

                            def opmm(e, ib=ib, sub=sub):
                                for k in range(32):
                                    ins = e.matmul(pA[:], yfc[ib][:, k * 256 + sub * 128: k * 256 + sub * 128 + 128],
                                                   wqb[:, k * 512:(k + 1) * 512],
                                                   start=(k == 0), stop=(k == 31))
                                return ins
                            sc.add("pe", opmm, reads=[f"yfc{ib}", "wqb"], writes=["pA"])

                            def fcmm(e, ib=ib, sub=sub):
                                for k in range(64):
                                    ins = e.matmul(pB[:], uc[ib][:, k * 256 + sub * 128: k * 256 + sub * 128 + 128],
                                                   f2b[:, k * 512:(k + 1) * 512],
                                                   start=(k == 0), stop=(k == 63))
                                return ins
                            sc.add("pe", fcmm, reads=[f"uc{ib}", "f2b"], writes=["pB"])
                            oi = ost[io]
                            sc.add("dve", (lambda e, oi=oi, i=i: e.tensor_scalar(
                                oi[:], pA[:], g_cols[:, i:i + 1], None, op0=ALU.mult)),
                                reads=["pA", "g_cols"], writes=[f"ost{io}"])
                            sc.add("dve", (lambda e, oi=oi: e.tensor_add(oi[:], oi[:], pB[:])),
                                   reads=["pB", f"ost{io}"], writes=[f"ost{io}"])
                            sc.add("dve", (lambda e, oi=oi, ib=ib, sub=sub: e.tensor_add(
                                oi[:], oi[:], rsb[ib][:, sub * 512:(sub + 1) * 512])),
                                reads=[f"rsb{ib}", f"ost{io}"], writes=[f"ost{io}"])
                            sc.add("sp", (lambda e, oi=oi, i=i, qq=qq: e.dma_start(
                                out=d_res[i * 128:(i + 1) * 128, qq * 512:(qq + 1) * 512], in_=oi[:])),
                                reads=[f"ost{io}"], writes=[f"dres{i//2}"], dma=True, stream=f"ost{io}")
                    sc.end()

        def emit_gateup(hbuf):
            with ExitStack() as stP:
                sb = lambda n, s, d: stP.enter_context(nc.sbuf_tensor(n, s, d))
                wgb = [sb(f"f_wg{i}", [128, KD * 128], BF16) for i in range(2)]
                wub = [sb(f"f_wu{i}", [128, KD * 128], BF16) for i in range(2)]
                gsb = [sb(f"f_g{i}", [128, 512], BF16) for i in range(2)]
                usb = [sb(f"f_u{i}", [128, 512], BF16) for i in range(2)]
                fsb = [sb(f"f_f{i}", [128, 512], BF16) for i in range(4)]
                sgc = sb("f_sg", [128, 64], F32)
                suc = sb("f_su", [128, 64], F32)
                sc.begin('p5a')
                sc.add("sp", lambda e: e.dma_start(out=sgc[:], in_=sg_c.ap()), writes=["sgc"], dma=True)
                sc.add("sp", lambda e: e.dma_start(out=suc[:], in_=su_c.ap()), writes=["suc"], dma=True)
                for j in range(64):
                    jb = j % 2
                    for (wt, dst, Rw) in ((wg_i, wgb[jb], f"fwg{jb}"), (wu_i, wub[jb], f"fwu{jb}")):
                        for half in range(2):
                            sc.add("pool", (lambda e, wt=wt, dst=dst, j=j, half=half: e.dma_start(
                                out=dst[:, half * 1024:(half + 1) * 1024],
                                in_=dap(wt, j * 128 + half * 8 * 128 * INTER,
                                        [[INTER, 128], [128 * INTER, 8], [1, 128]]))),
                                writes=[Rw], dma=True)
                    for tc in range(4):
                        def gmm(e, jb=jb, tc=tc):
                            for k in range(KD):
                                ins = e.matmul(pA[:], wgb[jb][:, k * 128:(k + 1) * 128],
                                               hbuf[:, k * T + tc * 512:k * T + (tc + 1) * 512],
                                               start=(k == 0), stop=(k == KD - 1))
                            return ins
                        sc.add("pe", gmm, reads=[f"fwg{jb}", "hbuf"], writes=["pA"])

                        def umm(e, jb=jb, tc=tc):
                            for k in range(KD):
                                ins = e.matmul(pB[:], wub[jb][:, k * 128:(k + 1) * 128],
                                               hbuf[:, k * T + tc * 512:k * T + (tc + 1) * 512],
                                               start=(k == 0), stop=(k == KD - 1))
                            return ins
                        sc.add("pe", umm, reads=[f"fwu{jb}", "hbuf"], writes=["pB"])
                        idx = j * 4 + tc
                        gg, uu, ff = gsb[idx % 2], usb[idx % 2], fsb[idx % 4]
                        Rg, Ru, Rf = f"fg{idx%2}", f"fu{idx%2}", f"ff{idx%4}"
                        sc.add("act", (lambda e, gg=gg, j=j: e.activation(gg[:], pA[:], AF.Silu,
                                                                         scale=sgc[:, j:j + 1])),
                               reads=["pA", "sgc"], writes=[Rg])
                        sc.add("act", (lambda e, uu=uu, j=j: e.activation(uu[:], pB[:], AF.Identity,
                                                                         scale=suc[:, j:j + 1])),
                               reads=["pB", "suc"], writes=[Ru])
                        sc.add("dve", (lambda e, ff=ff, gg=gg, uu=uu: e.tensor_mul(ff[:], gg[:], uu[:])),
                               reads=[Rg, Ru], writes=[Rf])
                        sc.add("sp", (lambda e, ff=ff, j=j, tc=tc: e.dma_start(
                            out=d_fuse[j * 128:(j + 1) * 128, tc * 512:(tc + 1) * 512], in_=ff[:])),
                            reads=[Rf], dma=True)
                sc.end()

        def emit_down():
            with ExitStack() as stP:
                sb = lambda n, s, d: stP.enter_context(nc.sbuf_tensor(n, s, d))
                wdb = sb("d_wd", [128, 64 * 512], BF16)
                fcol = [sb(f"d_f{i}", [128, 64 * 256], BF16) for i in range(2)]
                rsb = [sb(f"d_rs{i}", [128, 2 * 512], F32) for i in range(2)]
                ostg = [sb(f"d_o{i}", [128, 512], F32) for i in range(2)]
                sdb = sb("d_sd", [128, D], F32)
                sc.begin('p5b_pre')
                sc.add("sp", lambda e: e.dma_start(out=sdb[:], in_=dap(sd_r, 0, [[0, 128], [1, D]])),
                       writes=["sdb"], dma=True)
                sc.end()
                for qq in range(4):
                    sc.begin(f'p5b_q{qq}')
                    for gset in range(8):
                        sc.add("pool", (lambda e, qq=qq, gset=gset: e.dma_start(
                            out=wdb[:, gset * 4096:(gset + 1) * 4096],
                            in_=dap(wd_i, qq * 512 + gset * 8 * 128 * D,
                                    [[D, 128], [128 * D, 8], [1, 512]]))),
                            writes=["wdb"], dma=True)
                    for i2 in range(8):
                        ib = i2 % 2
                        for gset in range(8):
                            qeng = "act" if gset % 2 == 0 else "sp"
                            sc.add(qeng, (lambda e, i2=i2, ib=ib, gset=gset: e.dma_start(
                                out=fcol[ib][:, gset * 2048:(gset + 1) * 2048],
                                in_=dap(d_fuse, i2 * 256 + gset * 8 * 128 * T,
                                        [[T, 128], [128 * T, 8], [1, 256]]))),
                                writes=[f"fcol{ib}"], dma=True)
                        sc.add("sp", (lambda e, i2=i2, ib=ib, qq=qq: e.dma_start(
                            out=rsb[ib][:].rearrange("p (s n) -> p s n", s=2),
                            in_=dap(d_res, (i2 * 256) * D + qq * 512,
                                    [[D, 128], [128 * D, 2], [1, 512]]))),
                            writes=[f"drs{ib}"], dma=True)
                        for sub in range(2):
                            i = i2 * 2 + sub
                            io = i % 2

                            def dmm(e, ib=ib, sub=sub):
                                for k in range(64):
                                    ins = e.matmul(pA[:], fcol[ib][:, k * 256 + sub * 128: k * 256 + sub * 128 + 128],
                                                   wdb[:, k * 512:(k + 1) * 512],
                                                   start=(k == 0), stop=(k == 63))
                                return ins
                            sc.add("pe", dmm, reads=[f"fcol{ib}", "wdb"], writes=["pA"])
                            oo = ostg[io]
                            sc.add("dve", (lambda e, oo=oo, qq=qq: e.tensor_mul(
                                oo[:], pA[:], sdb[:, qq * 512:(qq + 1) * 512])),
                                reads=["pA", "sdb"], writes=[f"dost{io}"])
                            sc.add("dve", (lambda e, oo=oo, ib=ib, sub=sub: e.tensor_add(
                                oo[:], oo[:], rsb[ib][:, sub * 512:(sub + 1) * 512])),
                                reads=[f"drs{ib}", f"dost{io}"], writes=[f"dost{io}"])
                            sc.add("sp", (lambda e, oo=oo, i=i, qq=qq: e.dma_start(
                                out=out[i * 128:(i + 1) * 128, qq * 512:(qq + 1) * 512], in_=oo[:])),
                                reads=[f"dost{io}"], dma=True, stream=f"dost{io}")
                    sc.end()

        with ExitStack() as stO:
            sbO = lambda n, s, d: stO.enter_context(nc.sbuf_tensor(n, s, d))
            a_my = sbO("a_my", [NH, T], F32)
            ldt_my = sbO("ldt_my", [NH, T], F32)
            state = sbO("state", [NH, NH * 64], F32)

            with ExitStack() as stH:
                hbuf = stH.enter_context(nc.sbuf_tensor("hbuf1", [128, KD * T], BF16))
                stf = stH.enter_context(nc.sbuf_tensor("stf1", [NH, T], F32))
                if _on("P0"):
                    emit_hpass("p0", hbuf, xb.ap(), 0, affine=True, res_add=True)
                if _on("P1"):
                    emit_inproj("p1", hbuf, list(range(0, 80)) + [OT_DT],
                                [d_zsil, d_xf, d_Bf, d_C], stf)
                    emit_dtprep("dpm", stf, a_my, ldt_my)
                if _on("spill"):
                    sc.begin()
                    sc.add("sp", lambda e: e.dma_start(out=d_h.ap(), in_=hbuf[:]),
                           reads=["hbuf"], dma=True)
                    sc.end()

            with ExitStack() as stPP:
                a_pp = stPP.enter_context(nc.sbuf_tensor("a_pp", [NH, T], F32))
                ldt_pp = stPP.enter_context(nc.sbuf_tensor("ldt_pp", [NH, T], F32))
                with ExitStack() as stH:
                    hbuf = stH.enter_context(nc.sbuf_tensor("hbuf2", [128, KD * T], BF16))
                    stf = stH.enter_context(nc.sbuf_tensor("stf2", [NH, T], F32))
                    if _on("P0p"):
                        emit_hpass("pp", hbuf, xb.ap(), T, affine=True, res_add=False)
                    if _on("P15a"):
                        emit_inproj("pq", hbuf, list(range(OT_X, OT_C)) + [OT_DT],
                                    [None, d_xpf, d_Bpf, None], stf)
                        emit_dtprep("dpp", stf, a_pp, ldt_pp)

                sc.begin()
                sc.add("dve", lambda e: e.memset(state[:], 0.0), writes=["state"])
                sc.end()
                if _on("P15b"):
                    emit_scan("s15", False, a_pp, ldt_pp, d_xpf, d_Bpf, state)
            sc.begin()
            sc.add("dve", lambda e: e.tensor_scalar(state[:], state[:], rm_sb[:], None, op0=ALU.mult),
                   reads=["state", "rm"], writes=["state"])
            sc.end()

            if _on("P3a"):
                with ExitStack() as stH:
                    hbuf = stH.enter_context(nc.sbuf_tensor("hbuf3", [128, KD * T], BF16))
                    sc.begin()
                    sc.add("sp", lambda e: e.dma_start(out=hbuf[:], in_=d_h.ap()),
                           writes=["hbuf"], dma=True)
                    sc.end()
                    emit_mlp(hbuf)

            if _on("P2"):
                emit_scan("s2", True, a_my, ldt_my, d_xf, d_Bf, state)

        if _on("P23b"):
            emit_p23b()

        with ExitStack() as stH:
            hbuf = stH.enter_context(nc.sbuf_tensor("hbuf4", [128, KD * T], BF16))
            if _on("P4"):
                emit_hpass("p4", hbuf, d_res.ap(), 0, affine=False, res_add=False)
            if _on("P5a"):
                emit_gateup(hbuf)
        if _on("P5b"):
            emit_down()

        if taps:
            sc.begin()
            for (th, tout, shp) in taps:
                n = int(np.prod(shp))
                sc.add("sp", (lambda e, th=th, tout=tout, n=n: e.dma_start(
                    out=dap(tout, 0, [[1, n]]), in_=dap(th, 0, [[1, n]]))), dma=True)
            sc.end()

    return nc



def _consts():
    ident = np.eye(128, dtype=np.float32)
    tri = np.zeros((128, 128), np.float32)
    for s_ in range(128):
        tri[s_, s_:] = 1.0
    trineg = np.zeros((128, 128), np.float32)
    for s_ in range(128):
        trineg[s_, :s_] = -1e30
    ones = np.ones((1, 128), np.float32)
    return ident, tri, trineg, ones


def kernel(L0_attn_hidden_states, L0_attn_ln_w, L0_attn_ln_b, L0_attn_in_proj_w,
           L0_attn_dt_bias, L0_attn_A_log, L0_attn_Dp, L0_attn_norm_w,
           L0_attn_out_proj_w, L0_attn_fc1_w, L0_attn_fc1_b, L0_attn_fc2_w,
           L0_attn_fc2_b, L0_ffn_w_gate, L0_ffn_w_up, L0_ffn_w_down,
           L0_ffn_s_gate, L0_ffn_s_up, L0_ffn_s_down):
    global _last_results
    x = np.ascontiguousarray(np.asarray(L0_attn_hidden_states, dtype=np.float32))
    ident, tri, trineg, ones = _consts()
    shared = {
        "winT": np.ascontiguousarray(np.asarray(L0_attn_in_proj_w, np.float32).T),
        "woutT": np.ascontiguousarray(np.asarray(L0_attn_out_proj_w, np.float32).T),
        "fc1T": np.ascontiguousarray(np.asarray(L0_attn_fc1_w, np.float32).T),
        "fc2T": np.ascontiguousarray(np.asarray(L0_attn_fc2_w, np.float32).T),
        "lnw_r": np.ascontiguousarray(np.asarray(L0_attn_ln_w, np.float32)),
        "lnb_r": np.ascontiguousarray(np.asarray(L0_attn_ln_b, np.float32)),
        "dtb_c": np.ascontiguousarray(np.asarray(L0_attn_dt_bias, np.float32).reshape(NH, 1)),
        "alog_c": np.ascontiguousarray(np.asarray(L0_attn_A_log, np.float32).reshape(NH, 1)),
        "dp_r": np.ascontiguousarray(np.asarray(L0_attn_Dp, np.float32).reshape(1, NH)),
        "nw_cols": np.ascontiguousarray(np.asarray(L0_attn_norm_w, np.float32).reshape(32, 128).T),
        "fc1b_c": np.ascontiguousarray(np.asarray(L0_attn_fc1_b, np.float32).reshape(64, 128).T),
        "fc2b_r": np.ascontiguousarray(np.asarray(L0_attn_fc2_b, np.float32)),
        "wg_i": np.ascontiguousarray(np.asarray(L0_ffn_w_gate, np.int8)),
        "wu_i": np.ascontiguousarray(np.asarray(L0_ffn_w_up, np.int8)),
        "wd_i": np.ascontiguousarray(np.asarray(L0_ffn_w_down, np.int8)),
        "sg_c": np.ascontiguousarray(np.asarray(L0_ffn_s_gate, np.float32).reshape(64, 128).T),
        "su_c": np.ascontiguousarray(np.asarray(L0_ffn_s_up, np.float32).reshape(64, 128).T),
        "sd_r": np.ascontiguousarray(np.asarray(L0_ffn_s_down, np.float32)),
        "ident_d": ident, "tri_d": tri, "trineg_d": trineg, "ones_d": ones,
    }
    in_maps = []
    for c in range(8):
        b, H = c // 2, c % 2
        xc = x[b] if H == 0 else np.ascontiguousarray(np.roll(x[b], T, axis=0))
        m = dict(shared)
        m["xb"] = np.ascontiguousarray(xc)
        m["rm_c"] = np.full((NH, 1), 1.0 if H == 1 else 0.0, np.float32)
        in_maps.append(m)
    nc = build_nc()
    cores = list(range(8)) if _DEV_CORES is None else list(_DEV_CORES)
    res = run_bass_kernel_spmd(nc, [in_maps[c] for c in cores], core_ids=list(range(len(cores))))
    _last_results = res
    outp = np.zeros((BATCH, SEQ, D), np.float32)
    for i, c in enumerate(cores):
        b, H = c // 2, c % 2
        outp[b, H * T:(H + 1) * T, :] = res.results[i]["out"]
    return outp


# version 06
# speedup vs baseline: 1.2404x; 1.1441x over previous
import numpy as np
from contextlib import ExitStack

import concourse.bass as bass
import concourse.mybir as mybir
from concourse.bass_utils import run_bass_kernel_spmd

F32 = mybir.dt.float32
BF16 = mybir.dt.bfloat16
I8 = mybir.dt.int8
AF = mybir.ActivationFunctionType
ALU = mybir.AluOpType

D = 2048
T = 2048
SEQ = 4096
BATCH = 4
DIN = 4096
DXB = 1024
NH = 64
INTER = 8192
DPROJ = 10304
OT_X, OT_B, OT_C, OT_DT = 32, 40, 48, 80
NT = 16
KD = 16
EPS = 1e-5
GELU_K = 1.5957691216057308

_STOP_AFTER = None
_TAPS = ()
_DEV_CORES = None
_last_results = None

ENGS = ("pe", "act", "dve", "pool", "sp")
_EATTR = {"pe": "tensor", "act": "scalar", "dve": "vector", "pool": "gpsimd", "sp": "sync"}

_ORDER = ["prelude", "P0", "P1", "spill", "P0p", "P15a", "P15b",
          "P3a", "P2", "P23b", "P4", "P5a", "P5b"]


def _on(name):
    if _STOP_AFTER is None:
        return True
    return _ORDER.index(name) <= _ORDER.index(_STOP_AFTER)


def dap(t, offset, dims):
    return bass.AP(t, offset, [list(d) for d in dims])


class _Op:
    __slots__ = ("eng", "emit", "deps", "inck", "inc", "idx", "tgt", "isdma", "skey")

    def __init__(self, eng, emit, isdma, skey):
        self.eng = eng
        self.emit = emit
        self.deps = []
        self.inck = 16 if isdma else 1
        self.inc = False
        self.idx = -1
        self.tgt = None
        self.isdma = isdma
        self.skey = skey


class Sched:

    def __init__(self, nc, st):
        self.nc = nc
        self.st = st
        self.sem = {e: st.enter_context(nc.semaphore("sem_" + e)) for e in ENGS}
        self.cnt = {k: 0 for k in ENGS}
        self.blk_id = 0
        self.ops = None
        self.writer = {}
        self.readers = {}
        self.nsem = 0

    def begin(self, name=None):
        assert self.ops is None
        self.scope = name or f"b{self.blk_id}"
        self.ops = {e: [] for e in ENGS}

    def add(self, eng, emit, reads=(), writes=(), dma=False, deps=(), stream=None):
        skey = eng
        if dma:
            if stream is None:
                for r in writes:
                    stream = r
                    break
            if stream is None:
                for r in reads:
                    stream = r
                    break
            skey = (eng, stream)
            if skey not in self.sem:
                self.sem[skey] = self.st.enter_context(
                    self.nc.semaphore(f"dsem{self.nsem}"))
                self.nsem += 1
                self.cnt[skey] = 0
        op = _Op(eng, emit, dma, skey)
        dd = set()
        for r in reads:
            w = self.writer.get(r)
            if w is not None and w[0] == self.blk_id and w[1] is not op:
                dd.add(w[1])
            self.readers.setdefault(r, []).append((self.blk_id, op))
        for r in writes:
            w = self.writer.get(r)
            if w is not None and w[0] == self.blk_id:
                dd.add(w[1])
            for (b, rd) in self.readers.get(r, ()):
                if b == self.blk_id and rd is not op:
                    dd.add(rd)
            self.writer[r] = (self.blk_id, op)
            self.readers[r] = []
        for d in deps:
            if d is not None:
                dd.add(d)
        op.deps = list(dd)
        self.ops[eng].append(op)
        return op

    def end(self):
        ops = self.ops
        for e in ENGS:
            for op in ops[e]:
                if op.isdma:
                    op.inc = True
                for d in op.deps:
                    d.inc = True
        cnt = dict(self.cnt)
        for e in ENGS:
            for op in ops[e]:
                op.idx = len(ops[e])
        for e in ENGS:
            for i, op in enumerate(ops[e]):
                op.idx = i
                if op.inc:
                    cnt[op.skey] = cnt.get(op.skey, 0) + op.inck
                op.tgt = cnt.get(op.skey, 0)
        for e in ENGS:
            for op in ops[e]:
                for d in op.deps:
                    if d.eng == e:
                        assert d.idx < op.idx, "same-engine dep must precede"
        self.cnt.update(cnt)
        nc, sem = self.nc, self.sem
        with nc.named_scope(self.scope), nc.Block() as blk:
            for e in ENGS:
                if not ops[e]:
                    continue

                def body(eng_obj, e=e):
                    hw = {}
                    for op in ops[e]:
                        want = {}
                        for d in op.deps:
                            k = d.skey
                            want[k] = max(want.get(k, 0), d.tgt)
                        for k, v in want.items():
                            if v > hw.get(k, 0):
                                eng_obj.wait_ge(sem[k], v)
                                hw[k] = v
                        ins = op.emit(eng_obj)
                        if op.inc:
                            ins.then_inc(sem[op.skey], op.inck)
                    drains = {}
                    for o in ops[e]:
                        if o.isdma and o.inc:
                            drains[o.skey] = cnt[o.skey]
                    for k, v in drains.items():
                        if v > hw.get(k, 0):
                            eng_obj.wait_ge(sem[k], v)

                getattr(blk, _EATTR[e])(body)
        self.ops = None
        self.blk_id += 1


def build_nc():
    nc = bass.Bass("TRN2", target_bir_lowering=False)

    ein = lambda n, s, d: nc.dram_tensor(n, s, d, kind="ExternalInput")
    xb = ein("xb", [SEQ, D], F32)
    winT = ein("winT", [D, DPROJ], F32)
    lnw_r = ein("lnw_r", [D], F32)
    lnb_r = ein("lnb_r", [D], F32)
    dtb_c = ein("dtb_c", [NH, 1], F32)
    alog_c = ein("alog_c", [NH, 1], F32)
    dp_r = ein("dp_r", [1, NH], F32)
    nw_cols = ein("nw_cols", [128, 32], F32)
    woutT = ein("woutT", [DIN, D], F32)
    fc1T = ein("fc1T", [D, INTER], F32)
    fc1b_c = ein("fc1b_c", [128, 64], F32)
    fc2T = ein("fc2T", [INTER, D], F32)
    fc2b_r = ein("fc2b_r", [D], F32)
    wg_i = ein("wg_i", [D, INTER], I8)
    wu_i = ein("wu_i", [D, INTER], I8)
    wd_i = ein("wd_i", [INTER, D], I8)
    sg_c = ein("sg_c", [128, 64], F32)
    su_c = ein("su_c", [128, 64], F32)
    sd_r = ein("sd_r", [D], F32)
    ident_d = ein("ident_d", [128, 128], F32)
    tri_d = ein("tri_d", [128, 128], F32)
    trineg_d = ein("trineg_d", [128, 128], F32)
    ones_d = ein("ones_d", [1, 128], F32)
    rm_c = ein("rm_c", [NH, 1], F32)

    out = nc.dram_tensor("out", [T, D], F32, kind="ExternalOutput")

    d_res = nc.dram_tensor("d_res", [T, D], F32)
    d_h = nc.dram_tensor("d_h", [128, KD * T], BF16)
    d_zsil = nc.dram_tensor("d_zsil", [DIN, T], BF16)
    d_C = nc.dram_tensor("d_C", [DIN, T], BF16)
    d_xf = nc.dram_tensor("d_xf", [DXB, T], BF16)
    d_Bf = nc.dram_tensor("d_Bf", [DXB, T], BF16)
    d_xpf = nc.dram_tensor("d_xpf", [DXB, T], BF16)
    d_Bpf = nc.dram_tensor("d_Bpf", [DXB, T], BF16)
    d_expca = nc.dram_tensor("d_expca", [NH, T], BF16)
    d_yf = nc.dram_tensor("d_yf", [DIN, T], BF16)
    d_u = nc.dram_tensor("d_u", [INTER, T], BF16)
    d_fuse = nc.dram_tensor("d_fuse", [INTER, T], BF16)

    scrmap = {"d_res": (d_res, [T, D]), "d_zsil": (d_zsil, [DIN, T]),
              "d_C": (d_C, [DIN, T]), "d_xf": (d_xf, [DXB, T]),
              "d_Bf": (d_Bf, [DXB, T]), "d_xpf": (d_xpf, [DXB, T]),
              "d_Bpf": (d_Bpf, [DXB, T]), "d_yf": (d_yf, [DIN, T]),
              "d_u": (d_u, [INTER, T]), "d_fuse": (d_fuse, [INTER, T]),
              "d_h": (d_h, [128, KD * T]), "d_expca": (d_expca, [NH, T])}
    taps = []
    for name in _TAPS:
        th, shp = scrmap[name]
        dt = F32 if name == "d_res" else BF16
        taps.append((th, nc.dram_tensor("tap_" + name, shp, dt, kind="ExternalOutput"), shp))

    stG = ExitStack()
    with stG:
        sbG = lambda n, s, d: stG.enter_context(nc.sbuf_tensor(n, s, d))
        ps = lambda n, s, d: stG.enter_context(nc.psum_tensor(n, s, d))

        g_cols = sbG("g_cols", [128, NT], F32)
        ident_f = sbG("ident_f", [128, 128], F32)
        ident_b = sbG("ident_b", [128, 128], BF16)
        tri_f = sbG("tri_f", [128, 128], F32)
        trineg4_b = sbG("trineg4_b", [128, 512], BF16)
        ones_f = sbG("ones_f", [1, 128], F32)
        ones_b = sbG("ones_b", [1, 128], BF16)
        onescol_b = sbG("onescol_b", [128, 1], BF16)
        dtb_sb = sbG("dtb_sb", [NH, 1], F32)
        nega_sb = sbG("nega_sb", [NH, 1], F32)
        dp_sb = sbG("dp_sb", [1, NH], F32)
        dpcols = sbG("dpcols", [NH, NH], F32)
        rm_sb = sbG("rm_sb", [NH, 1], F32)
        cst_eps = sbG("cst_eps", [128, 1], F32)
        cst_one = sbG("cst_one", [128, 1], F32)

        pA = ps("pA", [128, 512], F32)
        pB = ps("pB", [128, 512], F32)
        pC = ps("pC", [128, 512], F32)
        pD = ps("pD", [64, 512], F32)
        pG = ps("pG", [128, 512], F32)
        pT2 = [ps("pT0", [128, 1024], BF16), ps("pT1", [128, 1024], BF16)]
        pF = ps("pF", [128, 512], F32)

        sc = Sched(nc, stG)

        sc.begin()
        sc.add("sp", lambda e: e.dma_start(out=ident_f[:], in_=ident_d.ap()), writes=["ident_f"], dma=True)
        sc.add("sp", lambda e: e.dma_start(out=tri_f[:], in_=tri_d.ap()), writes=["tri_f"], dma=True)
        sc.add("sp", lambda e: e.dma_start(out=ones_f[:], in_=ones_d.ap()), writes=["ones_f"], dma=True)
        sc.add("sp", lambda e: e.dma_start(out=dtb_sb[:], in_=dtb_c.ap()), writes=["dtb"], dma=True)
        sc.add("sp", lambda e: e.dma_start(out=rm_sb[:], in_=rm_c.ap()), writes=["rm"], dma=True)
        sc.add("sp", lambda e: e.dma_start(out=dp_sb[:], in_=dp_r.ap()), writes=["dp"], dma=True)
        sc.add("sp", lambda e: e.dma_start(out=nega_sb[:], in_=alog_c.ap()), writes=["nega"], dma=True)
        sc.add("pool", lambda e: e.dma_start(out=ident_b[:], in_=ident_d.ap()), writes=["ident_b"], dma=True)
        sc.add("pool", lambda e: e.dma_start(out=ones_b[:], in_=ones_d.ap()), writes=["ones_b"], dma=True)
        sc.add("pool", lambda e: e.dma_start(out=trineg4_b[:], in_=dap(trineg_d, 0, [[128, 128], [0, 4], [1, 128]])), writes=["trineg"], dma=True)
        for _q in range(4):
            sc.add("pool", (lambda e, _q=_q: e.dma_start(
                out=d_res[_q * 512:(_q + 1) * 512, :],
                in_=dap(fc2b_r, 0, [[0, 512], [1, D]]))), writes=["d_res"], dma=True)
        sc.add("dve", lambda e: e.memset(onescol_b[:], 1.0), writes=["onescol"])
        sc.add("dve", lambda e: e.memset(cst_eps[:], EPS), writes=["cst_eps"])
        sc.add("dve", lambda e: e.memset(cst_one[:], 1.0), writes=["cst_one"])
        nc.const_aps.register(F32, EPS, cst_eps[:])
        nc.const_aps.register(F32, 1.0, cst_one[:])
        sc.add("act", lambda e: e.activation(nega_sb[:], nega_sb[:], AF.Exp), reads=["nega"], writes=["nega"])
        sc.add("pe", lambda e: e.matmul(pG[0:NH, 0:NH], ones_f[0:1, 0:NH], dp_sb[:], start=True, stop=True),
               reads=["ones_f", "dp"], writes=["pG"])
        sc.add("dve", lambda e: e.tensor_copy(dpcols[:], pG[0:NH, 0:NH]), reads=["pG"], writes=["dpcols"])
        sc.end()

        def emit_hpass(tag, hbuf, src_dram, src_base, *, affine, res_add):
            with ExitStack() as stP:
                sb = lambda n, s, d: stP.enter_context(nc.sbuf_tensor(n, s, d))
                xt = [sb(f"{tag}_xt{i}", [128, D], F32) for i in range(2)]
                xm = sb(f"{tag}_xm", [128, D], F32)
                sq = sb(f"{tag}_sq", [128, D], F32)
                h0 = sb(f"{tag}_h0", [128, D], F32)
                rt = sb(f"{tag}_rt", [128, D], F32)
                hbf = [sb(f"{tag}_hbf{i}", [128, D], BF16) for i in range(2)]
                cols = sb(f"{tag}_cols", [128, 8], F32)
                junk = sb(f"{tag}_junk", [128, 1], F32)
                if affine:
                    lnwb = sb(f"{tag}_lnw", [128, D], F32)
                    lnbb = sb(f"{tag}_lnb", [128, D], F32)

                sc.begin(tag)
                if affine:
                    sc.add("sp", lambda e: e.dma_start(out=lnwb[:], in_=dap(lnw_r, 0, [[0, 128], [1, D]])),
                           writes=["lnwb"], dma=True)
                    sc.add("sp", lambda e: e.dma_start(out=lnbb[:], in_=dap(lnb_r, 0, [[0, 128], [1, D]])),
                           writes=["lnbb"], dma=True)
                for i in range(NT):
                    xti = xt[i % 2]
                    Rx = f"xt{i%2}"
                    sc.add("sp", (lambda e, i=i, xti=xti: e.dma_start(
                        out=xti[:], in_=src_dram[src_base + i * 128: src_base + (i + 1) * 128, :])),
                        writes=[Rx], dma=True)
                    sc.add("act", (lambda e, xti=xti: e.activation(sq[:], xti[:], AF.Copy, accum_out=cols[:, 0:1])),
                           reads=[Rx], writes=["sq", "c0"])
                    sc.add("dve", lambda e: e.tensor_scalar_mul(cols[:, 0:1], cols[:, 0:1], 1.0 / D),
                           reads=["c0"], writes=["c0"])
                    sc.add("dve", (lambda e, xti=xti: e.tensor_scalar(xm[:], xti[:], cols[:, 0:1], None, op0=ALU.subtract)),
                           reads=[Rx, "c0"], writes=["xm"])
                    sc.add("act", lambda e: e.activation(sq[:], xm[:], AF.Square, accum_out=cols[:, 1:2]),
                           reads=["xm"], writes=["sq", "c1"])
                    sc.add("act", lambda e: e.activation(cols[:, 2:3], cols[:, 1:2], AF.Ln, bias=EPS, scale=1.0 / D),
                           reads=["c1"], writes=["c2"])
                    sc.add("act", lambda e: e.activation(cols[:, 3:4], cols[:, 2:3], AF.Exp, scale=-0.5),
                           reads=["c2"], writes=["c3"])
                    sc.add("dve", lambda e: e.tensor_scalar(h0[:], xm[:], cols[:, 3:4], None, op0=ALU.mult),
                           reads=["xm", "c3"], writes=["h0"])
                    if res_add:
                        sc.add("dve", (lambda e, xti=xti: e.tensor_add(rt[:], xti[:], h0[:])),
                               reads=[Rx, "h0"], writes=["rt"])
                        sc.add("pool", (lambda e, i=i: e.dma_start(
                            out=d_res[i * 128:(i + 1) * 128, :], in_=rt[:], accum_op=ALU.add)),
                            reads=["rt"], dma=True)
                    if affine:
                        sc.add("act", lambda e: e.activation(sq[:], h0[:], AF.Copy, accum_out=cols[:, 4:5]),
                               reads=["h0"], writes=["sq", "c4"])
                        sc.add("dve", lambda e: e.tensor_scalar_mul(cols[:, 4:5], cols[:, 4:5], 1.0 / D),
                               reads=["c4"], writes=["c4"])
                        sc.add("dve", lambda e: e.tensor_scalar(xm[:], h0[:], cols[:, 4:5], None, op0=ALU.subtract),
                               reads=["h0", "c4"], writes=["xm"])
                        sc.add("act", lambda e: e.activation(sq[:], xm[:], AF.Square, accum_out=cols[:, 5:6]),
                               reads=["xm"], writes=["sq", "c5"])
                        sc.add("act", lambda e: e.activation(cols[:, 6:7], cols[:, 5:6], AF.Ln, bias=EPS, scale=1.0 / D),
                               reads=["c5"], writes=["c6"])
                        sc.add("act", lambda e: e.activation(cols[:, 7:8], cols[:, 6:7], AF.Exp, scale=-0.5),
                               reads=["c6"], writes=["c7"])
                        sc.add("dve", lambda e: e.tensor_scalar(h0[:], xm[:], cols[:, 7:8], None, op0=ALU.mult),
                               reads=["xm", "c7"], writes=["h0"])
                        sc.add("dve", lambda e: e.tensor_mul(h0[:], h0[:], lnwb[:]),
                               reads=["h0", "lnwb"], writes=["h0"])
                        hb = hbf[i % 2]
                        sc.add("dve", (lambda e, hb=hb: e.tensor_add(hb[:], h0[:], lnbb[:])),
                               reads=["h0", "lnbb"], writes=[f"hbf{i%2}"])
                    else:
                        hb = hbf[i % 2]
                        sc.add("dve", (lambda e, hb=hb: e.tensor_scalar(hb[:], xm[:], cols[:, 3:4], None, op0=ALU.mult)),
                               reads=["xm", "c3"], writes=[f"hbf{i%2}"])
                    for k in range(KD):
                        hb = hbf[i % 2]
                        slot = k % 2
                        sc.add("pe", (lambda e, hb=hb, k=k, slot=slot: e.transpose(
                            pT2[slot][:, 0:128], hb[:, k * 128:(k + 1) * 128], ident_b[:])),
                            reads=[f"hbf{i%2}", "ident_b"], writes=[f"pT{slot}"])
                        sc.add("act", (lambda e, i=i, k=k, slot=slot: e.copy(
                            hbuf[:, k * T + i * 128: k * T + (i + 1) * 128],
                            pT2[slot][:, 0:128])),
                            reads=[f"pT{slot}"], writes=["hbuf"])
                sc.end()

        def emit_inproj(tag, hbuf, otiles, dsts, stf):
            with ExitStack() as stP:
                sb = lambda n, s, d: stP.enter_context(nc.sbuf_tensor(n, s, d))
                wb = [sb(f"{tag}_wb{i}", [128, KD * 128], BF16) for i in range(2)]
                stg = [sb(f"{tag}_st{i}", [128, 512], BF16) for i in range(4)]
                sc.begin(tag)
                for j, ot in enumerate(otiles):
                    wbj = wb[j % 2]
                    Rw = f"wb{j%2}"
                    wcols = 64 if ot == OT_DT else 128
                    for half in range(2):
                        sc.add("pool", (lambda e, wbj=wbj, ot=ot, half=half, wcols=wcols: e.dma_start(
                            out=wbj[:, half * 1024:(half + 1) * 1024].rearrange(
                                "p (k o) -> p k o", k=8)[:, :, 0:wcols],
                            in_=dap(winT, ot * 128 + half * 8 * 128 * DPROJ,
                                    [[DPROJ, 128], [128 * DPROJ, 8], [1, wcols]]))),
                            writes=[Rw], dma=True)
                    m = 64 if ot == OT_DT else 128
                    for tc in range(4):
                        p = pA if tc % 2 == 0 else pB
                        Rp = "pA" if tc % 2 == 0 else "pB"

                        def mmg(e, wbj=wbj, p=p, m=m, tc=tc):
                            for k in range(KD):
                                ins = e.matmul(p[0:m, :], wbj[0:128, k * 128: k * 128 + m],
                                               hbuf[:, k * T + tc * 512: k * T + (tc + 1) * 512],
                                               start=(k == 0), stop=(k == KD - 1))
                            return ins
                        sc.add("pe", mmg, reads=[Rw, "hbuf"], writes=[Rp])
                        idx = j * 4 + tc
                        sg = stg[idx % 4]
                        Rs = f"stg{idx%4}"
                        if ot == OT_DT:
                            sc.add("act", (lambda e, p=p, tc=tc: e.copy(
                                stf[0:64, tc * 512:(tc + 1) * 512], p[0:64, :])),
                                reads=[Rp], writes=["stf"])
                            continue
                        if ot < OT_X:
                            sc.add("act", (lambda e, sg=sg, p=p: e.activation(sg[:], p[:], AF.Silu)),
                                   reads=[Rp], writes=[Rs])
                            dst, row = dsts[0], ot * 128
                        else:
                            sc.add("act", (lambda e, sg=sg, p=p: e.copy(sg[:], p[:])),
                                   reads=[Rp], writes=[Rs])
                            if ot < OT_B:
                                dst, row = dsts[1], (ot - OT_X) * 128
                            elif ot < OT_C:
                                dst, row = dsts[2], (ot - OT_B) * 128
                            else:
                                dst, row = dsts[3], (ot - OT_C) * 128
                        sc.add("sp", (lambda e, dst=dst, row=row, tc=tc, sg=sg: e.dma_start(
                            out=dst[row:row + 128, tc * 512:(tc + 1) * 512], in_=sg[:])),
                            reads=[Rs], dma=True)
                sc.end()

        def emit_dtprep(tag, stf, a_dst, ldt_dst):
            with ExitStack() as stP:
                sb = lambda n, s, d: stP.enter_context(nc.sbuf_tensor(n, s, d))
                t0 = sb(f"{tag}_t0", [NH, T], F32)
                t1 = sb(f"{tag}_t1", [NH, T], F32)
                sc.begin()
                sc.add("dve", lambda e: e.tensor_scalar(t0[:], stf[:], dtb_sb[:], None, op0=ALU.add),
                       reads=["stf", "dtb"], writes=["t0"])
                sc.add("act", lambda e: e.activation(t1[:], t0[:], AF.Abs), reads=["t0"], writes=["t1"])
                sc.add("act", lambda e: e.activation(t1[:], t1[:], AF.Exp, scale=-1.0), reads=["t1"], writes=["t1"])
                sc.add("act", lambda e: e.activation(t1[:], t1[:], AF.Ln, bias=1.0), reads=["t1"], writes=["t1"])
                sc.add("dve", lambda e: e.tensor_scalar_max(stf[:], t0[:], 0.0), reads=["t0"], writes=["stf"])
                sc.add("dve", lambda e: e.tensor_add(stf[:], stf[:], t1[:]), reads=["stf", "t1"], writes=["stf"])
                sc.add("dve", (lambda e: e.tensor_scalar(a_dst[:], stf[:], nega_sb[:], -1.0,
                                                         op0=ALU.mult, op1=ALU.mult)),
                       reads=["stf", "nega"], writes=["a_dst"])
                sc.add("act", lambda e: e.activation(ldt_dst[:], stf[:], AF.Ln), reads=["stf"], writes=["ldt_dst"])
                sc.end()

        def emit_scan(tag, full, a_sb, ldt_sb, src_x, src_B, state):
            with ExitStack() as stP:
                sb = lambda n, s, d: stP.enter_context(nc.sbuf_tensor(n, s, d))
                xcf2 = [sb(f"{tag}_xcf{pb}", [128, 8 * 256], BF16) for pb in range(2)]
                Bcf2 = [[sb(f"{tag}_Bcf{p}{pb}", [64, 8 * 256], BF16) for pb in range(2)]
                        for p in range(2)]
                pbof = lambda c: (c // 2) % 2
                xcf_sl = lambda c, k: xcf2[pbof(c)][:, (2 * k + c % 2) * 128:(2 * k + c % 2 + 1) * 128]
                Bcf_sl = lambda par, c, k: Bcf2[par][pbof(c)][0:64, (2 * k + c % 2) * 128:(2 * k + c % 2 + 1) * 128]
                xct = sb(f"{tag}_xct", [128, 8 * 128], BF16)
                Bct = sb(f"{tag}_Bct", [128, 8 * 128], BF16)
                aT = sb(f"{tag}_aT", [128, 64], F32)
                ldtT = sb(f"{tag}_ldtT", [128, 64], F32)
                caT = sb(f"{tag}_caT", [128, 64], F32)
                colB = sb(f"{tag}_colB", [128, 64], F32)
                calast = sb(f"{tag}_cal", [1, 64], F32)
                dec_b = sb(f"{tag}_dec", [1, 64], BF16)
                dec_cols = sb(f"{tag}_decc", [NH, NH], F32)
                wall = sb(f"{tag}_wall", [128, 64], F32)
                btil = [sb(f"{tag}_btl{i}", [128, 256], BF16) for i in range(2)]
                if full:
                    ca_fm = sb(f"{tag}_cafm", [64, 128], F32)
                    Cc2 = [[sb(f"{tag}_Cc{p}{pb}", [64, 32 * 256], BF16) for pb in range(2)]
                           for p in range(2)]
                    zc2 = [sb(f"{tag}_zc0", [128, 32 * 256], BF16)]
                    Cc_sl = lambda par, c, k: Cc2[par][pbof(c)][0:64, (2 * k + c % 2) * 128:(2 * k + c % 2 + 1) * 128]
                    Cc3 = lambda par, c: Cc2[par][pbof(c)][:].rearrange("p (k s t) -> p k s t", k=32, s=2)[0:64, :, c % 2, :]
                    zc3 = lambda c: zc2[0][:].rearrange("p (k s t) -> p k s t", k=32, s=2)[:, :, c % 2, :]
                    cbc = [sb(f"{tag}_cbc{p}", [64, 32 * 128], BF16) for p in range(2)]
                    chat = cbc
                    e4 = sb(f"{tag}_e4", [128, 512], BF16)
                    p4 = [sb(f"{tag}_p4{i}", [128, 512], BF16) for i in range(2)]
                    dpx = sb(f"{tag}_dpx", [128, 32 * 128], BF16)
                    ypre = sb(f"{tag}_ypre", [128, 32 * 128], BF16)
                    yf = sb(f"{tag}_yf", [128, 32 * 128], BF16)
                    cam = sb(f"{tag}_cam", [1, 1024], F32)
                    expca = sb(f"{tag}_expca", [64, 128], F32)
                    grow = sb(f"{tag}_grow", [1, 128], F32)
                    st_bf = sb(f"{tag}_stbf", [NH, NH * 64], BF16)
                    nw_sb = sb(f"{tag}_nw", [128, 32], F32)

                def loads(c):
                    pb = pbof(c)
                    for half in range(2):
                        sc.add("sp", (lambda e, c=c, pb=pb, half=half: e.dma_start(
                            out=xcf2[pb][:, half * 1024:(half + 1) * 1024].rearrange("p (k t) -> p k t", k=4),
                            in_=dap(src_x, c * 128 + half * 4 * 128 * T,
                                    [[T, 128], [128 * T, 4], [1, 256]]))),
                            writes=[f"xcf{pb}"], dma=True)
                    for par in range(2):
                        sc.add("sp", (lambda e, c=c, pb=pb, par=par: e.dma_start(
                            out=Bcf2[par][pb][:].rearrange("p (k t) -> p k t", k=8),
                            in_=dap(src_B, c * 128 + par * 64 * T,
                                    [[T, 64], [128 * T, 8], [1, 256]]))),
                            writes=[f"Bcf{par}{pb}"], dma=True)
                    if full:
                        for par in range(2):
                            for hf in range(2):
                                sc.add("sp", (lambda e, c=c, pb=pb, par=par, hf=hf: e.dma_start(
                                    out=Cc2[par][pb][:, hf * 4096:(hf + 1) * 4096].rearrange("p (k t) -> p k t", k=16),
                                    in_=dap(d_C, c * 128 + par * 64 * T + hf * 16 * 128 * T,
                                            [[T, 64], [128 * T, 16], [1, 256]]))),
                                    writes=[f"Cc{par}{pb}"], dma=True)


                sc.begin(tag + '_pre')
                if full:
                    sc.add("sp", lambda e: e.dma_start(out=nw_sb[:], in_=nw_cols.ap()),
                           writes=["nw"], dma=True)
                loads(0)
                sc.end()


                for c in range(NT):
                    cb = c % 2
                    sc.begin(f'{tag}_c{c}')
                    if full and c % 2 == 0:
                        for qr in range(4):
                            sc.add("sp", (lambda e, c=c, qr=qr: e.dma_start(
                                out=zc2[0][:, qr * 2048:(qr + 1) * 2048].rearrange("p (k t) -> p k t", k=8),
                                in_=dap(d_zsil, c * 128 + qr * 8 * 128 * T,
                                        [[T, 128], [128 * T, 8], [1, 256]]))),
                                writes=["zc0"], dma=True)
                    if c % 2 == 0 and c + 2 < NT:
                        loads(c + 2)
                    for k in range(8):
                        slot = k % 2
                        sc.add("pe", (lambda e, cb=cb, k=k, slot=slot: e.transpose(
                            pT2[slot][:, 0:128], xcf_sl(c, k), ident_b[:])),
                            reads=[f"xcf{pbof(c)}", "ident_b"], writes=[f"pT{slot}"])
                        sc.add("dve", (lambda e, k=k, slot=slot: e.tensor_copy(
                            xct[:, k * 128:(k + 1) * 128], pT2[slot][:, 0:128])),
                            reads=[f"pT{slot}"], writes=["xct"])
                    for g in range(16):
                        par, jj = g % 2, g // 2
                        slot = g % 2
                        sc.add("pe", (lambda e, cb=cb, par=par, jj=jj, slot=slot: e.transpose(
                            pT2[slot][0:128, 0:64],
                            Bcf_sl(par, c, jj), ident_b[0:64, 0:64])),
                            reads=[f"Bcf{par}{pbof(c)}", "ident_b"], writes=[f"pT{slot}"])
                        sc.add("dve", (lambda e, g=g, slot=slot: e.tensor_copy(
                            Bct[:, g * 64:(g + 1) * 64], pT2[slot][0:128, 0:64])),
                            reads=[f"pT{slot}"], writes=["Bct"])
                    sc.add("pe", (lambda e, c=c: e.transpose(pF[0:128, 0:64],
                                                             a_sb[0:64, c * 128:(c + 1) * 128],
                                                             ident_f[0:64, 0:64])),
                           reads=["a_sb", "ident_f"], writes=["pF"])
                    sc.add("dve", lambda e: e.tensor_copy(aT[:, 0:64], pF[0:128, 0:64]),
                           reads=["pF"], writes=["aT"])
                    sc.add("pe", (lambda e, c=c: e.transpose(pF[0:128, 0:64],
                                                             ldt_sb[0:64, c * 128:(c + 1) * 128],
                                                             ident_f[0:64, 0:64])),
                           reads=["ldt_sb", "ident_f", "aT"], writes=["pF"])
                    sc.add("dve", lambda e: e.tensor_copy(ldtT[:, 0:64], pF[0:128, 0:64]),
                           reads=["pF"], writes=["ldtT"])
                    sc.add("pe", lambda e: e.matmul(pG[0:128, 0:64], tri_f[:], aT[:, 0:64], start=True, stop=True),
                           reads=["tri_f", "aT"], writes=["pG"])
                    sc.add("dve", lambda e: e.tensor_copy(caT[:, 0:64], pG[0:128, 0:64]),
                           reads=["pG"], writes=["caT"])
                    sc.add("dve", lambda e: e.tensor_sub(colB[:, 0:64], ldtT[:, 0:64], caT[:, 0:64]),
                           reads=["ldtT", "caT"], writes=["colB"])
                    sc.add("act", lambda e: e.dma_start(out=calast[:], in_=caT[127:128, 0:64]),
                           reads=["caT"], writes=["calast"], dma=True)
                    sc.add("act", lambda e: e.activation(dec_b[:], calast[:], AF.Exp),
                           reads=["calast"], writes=["dec_b"])

                    def wallmm(e):
                        e.matmul(pG[0:128, 64:128], ones_f[:], calast[:], start=True, stop=False)
                        return e.matmul(pG[0:128, 64:128], ident_f[:], colB[:, 0:64], start=False, stop=True)
                    sc.add("pe", wallmm, reads=["ones_f", "calast", "ident_f", "colB"], writes=["pG"])
                    sc.add("act", lambda e: e.activation(wall[:, 0:64], pG[0:128, 64:128], AF.Exp),
                           reads=["pG"], writes=["wall"])
                    sc.add("pe", lambda e: e.matmul(pG[0:NH, 128:192], ones_b[0:1, 0:NH], dec_b[:], start=True, stop=True),
                           reads=["ones_b", "dec_b"], writes=["pG"])
                    sc.add("dve", lambda e: e.tensor_copy(dec_cols[:, 0:NH], pG[0:NH, 128:192]),
                           reads=["pG"], writes=["dec_cols"])

                    if full:
                        sc.add("pe", (lambda e: e.transpose(pF[0:64, 0:128], caT[0:128, 0:64], ident_f[:])),
                               reads=["caT", "ident_f", "ldtT"], writes=["pF"])
                        sc.add("dve", lambda e: e.tensor_copy(ca_fm[:], pF[0:64, 0:128]),
                               reads=["pF"], writes=["ca_fm"])
                        sc.add("act", lambda e: e.activation(expca[:], ca_fm[:], AF.Exp),
                               reads=["ca_fm"], writes=["expca"])
                        sc.add("pool", (lambda e, c=c: e.dma_start(
                            out=d_expca[0:64, c * 128:(c + 1) * 128], in_=expca[:])),
                            reads=["expca"], writes=["d_expca"], dma=True)
                        for par in range(2):
                            sc.add("pool", (lambda e, c=c, par=par: e.dma_start(
                                out=cbc[par][:], in_=dap(d_expca, par * T + c * 128,
                                                         [[0, 64], [2 * T, 32], [1, 128]]))),
                                reads=["d_expca"], writes=[f"cbc{par}"], dma=True)
                            sc.add("dve", (lambda e, c=c, par=par: e.tensor_tensor(
                                chat[par][:].rearrange("p (k t) -> p k t", k=32),
                                Cc3(par, c),
                                cbc[par][:].rearrange("p (k t) -> p k t", k=32), op=ALU.mult)),
                                reads=[f"Cc{par}{pbof(c)}", f"cbc{par}"], writes=[f"chat{par}", f"cbc{par}"])
                        sc.add("dve", lambda e: e.tensor_copy(st_bf[:], state[:]),
                               reads=["state"], writes=["st_bf"])

                    for q in range(16):
                        hs = [4 * q + i for i in range(4)]
                        g = q
                        gb, gt = 64 * (g % 2), (g // 2) * 128
                        if full:
                            if q % 2 == 0:
                                aa = q // 2
                                sc.add("pool", (lambda e, aa=aa: e.dma_start(
                                    out=cam[:], in_=ca_fm[aa * 8:(aa + 1) * 8, :])),
                                    reads=["ca_fm"], writes=["cam"], dma=True)

                            def d2mm(e, q=q):
                                e.matmul(pA[:], ones_f[:], cam[0:1, (q % 2) * 512:(q % 2 + 1) * 512],
                                         start=True, stop=False)
                                return e.matmul(pA[:], ident_b[:], trineg4_b[:], start=False, stop=True)
                            sc.add("pe", d2mm, reads=["cam", "ones_f", "ident_b", "trineg"], writes=["pA"])

                            def spmm(e, hs=hs, g=g, c=c):
                                for i, h in enumerate(hs):
                                    ins = e.matmul(pB[:, i * 128:(i + 1) * 128],
                                                   Bcf_sl(g % 2, c, g // 2),
                                                   Cc_sl(h % 2, c, h // 2),
                                                   start=True, stop=True)
                                return ins
                            sc.add("pe", spmm,
                                   reads=[f"Bcf{g%2}{pbof(c)}", f"Cc0{pbof(c)}", f"Cc1{pbof(c)}"], writes=["pB"])
                            for i, h in enumerate(hs):
                                sc.add("act", (lambda e, i=i, h=h: e.activation(
                                    e4[:, i * 128:(i + 1) * 128], pA[:, i * 128:(i + 1) * 128],
                                    AF.Exp, bias=colB[:, h:h + 1])),
                                    reads=["pA", "colB"], writes=["e4"])
                            p4q = p4[q % 2]
                            sc.add("dve", (lambda e, p4q=p4q: e.tensor_mul(p4q[:], pB[:], e4[:])),
                                   reads=["pB", "e4"], writes=[f"p4_{q%2}"])

                            def ymm(e, hs=hs, g=g, p4q=p4q):
                                for i, h in enumerate(hs):
                                    bp = 64 * (h % 2)
                                    col = ((h % 8) // 2) * 128
                                    e.matmul(pC[bp:bp + 64, col:col + 128],
                                             xct[:, g * 64:g * 64 + 64],
                                             p4q[:, i * 128:(i + 1) * 128],
                                             start=True, stop=False, tile_position=(0, bp))
                                    ins = e.matmul(pC[bp:bp + 64, col:col + 128],
                                                   st_bf[0:64, h * 64:(h + 1) * 64],
                                                   chat[h % 2][0:64,
                                                               (h // 2) * 128:(h // 2) * 128 + 128],
                                                   start=False, stop=True, tile_position=(0, bp))
                                return ins
                            sc.add("pe", ymm, reads=[f"p4_{q%2}", "xct", "st_bf", "chat0", "chat1"],
                                   writes=["pC"])
                            if q % 2 == 1:
                                def dpxb(e, q=q, c=c):
                                    for h in range(4 * q - 4, 4 * q + 4):
                                        gg = h // 4
                                        bp = 64 * (h % 2)
                                        col = (h // 2) * 128
                                        xb_ = 64 * (gg % 2)
                                        ins = e.tensor_scalar(dpx[bp:bp + 64, col:col + 128],
                                                              xcf_sl(c, gg // 2)[xb_:xb_ + 64, :],
                                                              dpcols[:, h:h + 1], None, op0=ALU.mult)
                                    return ins
                                sc.add("dve", dpxb, reads=[f"xcf{pbof(c)}", "dpcols"], writes=["dpx"])
                                a = q // 2
                                sc.add("dve", (lambda e, a=a: e.tensor_add(
                                    ypre[:, a * 512:(a + 1) * 512], dpx[:, a * 512:(a + 1) * 512], pC[:])),
                                    reads=["dpx", "pC"], writes=["ypre"])
                        btq = btil[q % 2]

                        def btb(e, hs=hs, g=g, btq=btq):
                            for i, h in enumerate(hs):
                                ins = e.tensor_scalar(btq[:, i * 64:(i + 1) * 64],
                                                      Bct[:, g * 64:g * 64 + 64],
                                                      wall[:, h:h + 1], None, op0=ALU.mult)
                            return ins
                        sc.add("dve", btb, reads=["Bct", "wall"], writes=[f"btil{q%2}"])

                        def saddmm(e, hs=hs, g=g, btq=btq):
                            for i, h in enumerate(hs):
                                ins = e.matmul(pD[0:64, (h % 8) * 64:(h % 8) * 64 + 64],
                                               btq[:, i * 64:(i + 1) * 64],
                                               xct[:, g * 64:g * 64 + 64],
                                               start=True, stop=True)
                            return ins
                        sc.add("pe", saddmm, reads=[f"btil{q%2}", "xct"], writes=["pD"])
                        if q % 2 == 1:
                            def stt(e, q=q):
                                for h in range(4 * q - 4, 4 * q + 4):
                                    ins = e.scalar_tensor_tensor(
                                        state[0:64, h * 64:(h + 1) * 64],
                                        state[0:64, h * 64:(h + 1) * 64],
                                        dec_cols[:, h:h + 1],
                                        pD[0:64, (h % 8) * 64:(h % 8) * 64 + 64],
                                        op0=ALU.mult, op1=ALU.add)
                                return ins
                            sc.add("dve", stt, reads=["pD", "dec_cols", "state"], writes=["state"])

                    if full:
                        sc.add("dve", (lambda e, c=c: e.tensor_tensor(
                            yf[:].rearrange("p (k t) -> p k t", k=32),
                            ypre[:].rearrange("p (k t) -> p k t", k=32),
                            zc3(c), op=ALU.mult)),
                            reads=["ypre", "zc0"], writes=["yf"])
                        sc.add("dve", lambda e: e.tensor_mul(dpx[:], yf[:], yf[:]),
                               reads=["yf"], writes=["dpx"])
                        nwrep = bass.AP(nw_sb.tensor if hasattr(nw_sb, "tensor") else nw_sb,
                                        0, [[32, 128], [1, 32], [0, 128]])
                        sc.add("dve", (lambda e, nwrep=nwrep: e.tensor_tensor(
                            ypre[:].rearrange("p (j t) -> p j t", j=32), yf[:].rearrange("p (j t) -> p j t", j=32),
                            nwrep, op=ALU.mult)),
                            reads=["yf", "nw"], writes=["ypre"])

                        def ssmm(e):
                            for k in range(32):
                                ins = e.matmul(pG[0:1, 192:320], onescol_b[:],
                                               dpx[:, k * 128:(k + 1) * 128],
                                               start=(k == 0), stop=(k == 31))
                            return ins
                        sc.add("pe", ssmm, reads=["dpx", "onescol"], writes=["pG"])
                        sc.add("act", lambda e: e.activation(grow[:], pG[0:1, 192:320], AF.Ln,
                                                             scale=1.0 / DIN, bias=EPS),
                               reads=["pG"], writes=["grow"])
                        sc.add("act", lambda e: e.activation(grow[:], grow[:], AF.Exp, scale=-0.5),
                               reads=["grow"], writes=["grow"])
                        sc.add("pe", lambda e: e.matmul(pG[0:128, 320:321], grow[:], ones_f[0:1, 0:1],
                                                        start=True, stop=True),
                               reads=["grow", "ones_f"], writes=["pG"])
                        sc.add("dve", (lambda e, c=c: e.tensor_copy(g_cols[:, c:c + 1], pG[0:128, 320:321])),
                               reads=["pG"], writes=["g_cols"])
                        for qr in range(4):
                            sc.add("pool", (lambda e, c=c, qr=qr: e.dma_start(
                                out=dap(d_yf, c * 128 + qr * 8 * 128 * T, [[T, 128], [128 * T, 8], [1, 128]]),
                                in_=ypre[:, qr * 1024:(qr + 1) * 1024])),
                                reads=["ypre"], writes=["d_yf"], dma=True)
                    sc.end()

        def emit_mlp(hbuf):
            with ExitStack() as stP:
                sb = lambda n, s, d: stP.enter_context(nc.sbuf_tensor(n, s, d))
                wb = [sb(f"m_wb{i}", [128, KD * 128], BF16) for i in range(2)]
                t0 = [sb(f"m_t0{i}", [128, 512], F32) for i in range(2)]
                t1 = [sb(f"m_t1{i}", [128, 512], F32) for i in range(2)]
                stg = [sb(f"m_st{i}", [128, 512], BF16) for i in range(4)]
                fb = sb("m_fb", [128, 64], F32)
                sc.begin('mlp')
                sc.add("sp", lambda e: e.dma_start(out=fb[:], in_=fc1b_c.ap()), writes=["fb"], dma=True)
                for j in range(64):
                    wbj = wb[j % 2]
                    Rw = f"mwb{j%2}"
                    for half in range(2):
                        sc.add("pool", (lambda e, wbj=wbj, j=j, half=half: e.dma_start(
                            out=wbj[:, half * 1024:(half + 1) * 1024],
                            in_=dap(fc1T, j * 128 + half * 8 * 128 * INTER,
                                    [[INTER, 128], [128 * INTER, 8], [1, 128]]))),
                            writes=[Rw], dma=True)
                    for tc in range(4):
                        p = pA if tc % 2 == 0 else pB
                        Rp = "pA" if tc % 2 == 0 else "pB"

                        def mmg(e, wbj=wbj, p=p, tc=tc):
                            for k in range(KD):
                                ins = e.matmul(p[:], wbj[:, k * 128:(k + 1) * 128],
                                               hbuf[:, k * T + tc * 512: k * T + (tc + 1) * 512],
                                               start=(k == 0), stop=(k == KD - 1))
                            return ins
                        sc.add("pe", mmg, reads=[Rw, "hbuf"], writes=[Rp])
                        idx = j * 4 + tc
                        tt0, tt1, sg = t0[idx % 2], t1[idx % 2], stg[idx % 4]
                        R0, R1, Rs = f"mt0{idx%2}", f"mt1{idx%2}", f"mst{idx%4}"
                        sc.add("act", (lambda e, tt0=tt0, p=p, j=j: e.activation(
                            tt0[:], p[:], AF.Identity, bias=fb[:, j:j + 1])),
                            reads=[Rp, "fb"], writes=[R0])
                        sc.add("dve", (lambda e, tt0=tt0, tt1=tt1: e.tensor_mul(tt1[:], tt0[:], tt0[:])),
                               reads=[R0], writes=[R1])
                        sc.add("dve", (lambda e, tt1=tt1: e.tensor_scalar(tt1[:], tt1[:], 0.044715, 1.0,
                                                                          op0=ALU.mult, op1=ALU.add)),
                               reads=[R1], writes=[R1])
                        sc.add("dve", (lambda e, tt0=tt0, tt1=tt1: e.tensor_mul(tt1[:], tt1[:], tt0[:])),
                               reads=[R0, R1], writes=[R1])
                        sc.add("act", (lambda e, tt1=tt1: e.activation(tt1[:], tt1[:], AF.Sigmoid, scale=GELU_K)),
                               reads=[R1], writes=[R1])
                        sc.add("dve", (lambda e, sg=sg, tt0=tt0, tt1=tt1: e.tensor_mul(sg[:], tt1[:], tt0[:])),
                               reads=[R0, R1], writes=[Rs])
                        sc.add("sp", (lambda e, sg=sg, j=j, tc=tc: e.dma_start(
                            out=d_u[j * 128:(j + 1) * 128, tc * 512:(tc + 1) * 512], in_=sg[:])),
                            reads=[Rs], dma=True)
                sc.end()

        def emit_p23b():
            with ExitStack() as stP:
                sb = lambda n, s, d: stP.enter_context(nc.sbuf_tensor(n, s, d))
                wqb = sb("o_wqb", [128, 32 * 512], BF16)
                f2b = sb("o_f2b", [128, 64 * 512], BF16)
                yfc = [sb(f"o_yf{i}", [128, 32 * 256], BF16) for i in range(2)]
                uc = [sb(f"o_uc{i}", [128, 64 * 256], BF16) for i in range(2)]
                rsb = [sb(f"o_rs{i}", [128, 2 * 512], F32) for i in range(2)]
                ost = [sb(f"o_ost{i}", [128, 512], F32) for i in range(2)]
                for qq in range(4):
                    sc.begin(f'p23b_q{qq}')
                    for gset in range(4):
                        sc.add("pool", (lambda e, qq=qq, gset=gset: e.dma_start(
                            out=wqb[:, gset * 4096:(gset + 1) * 4096],
                            in_=dap(woutT, qq * 512 + gset * 8 * 128 * D,
                                    [[D, 128], [128 * D, 8], [1, 512]]))),
                            writes=["wqb"], dma=True)
                    for gset in range(8):
                        sc.add("pool", (lambda e, qq=qq, gset=gset: e.dma_start(
                            out=f2b[:, gset * 4096:(gset + 1) * 4096],
                            in_=dap(fc2T, qq * 512 + gset * 8 * 128 * D,
                                    [[D, 128], [128 * D, 8], [1, 512]]))),
                            writes=["f2b"], dma=True)
                    def p23_loads(i2):
                        ib = i2 % 2
                        for gset in range(4):
                            sc.add("sp", (lambda e, i2=i2, ib=ib, gset=gset: e.dma_start(
                                out=yfc[ib][:, gset * 2048:(gset + 1) * 2048],
                                in_=dap(d_yf, i2 * 256 + gset * 8 * 128 * T,
                                        [[T, 128], [128 * T, 8], [1, 256]]))),
                                writes=[f"yfc{ib}"], dma=True)
                        for gset in range(8):
                            qeng = "act" if gset % 2 == 0 else "sp"
                            sc.add(qeng, (lambda e, i2=i2, ib=ib, gset=gset: e.dma_start(
                                out=uc[ib][:, gset * 2048:(gset + 1) * 2048],
                                in_=dap(d_u, i2 * 256 + gset * 8 * 128 * T,
                                        [[T, 128], [128 * T, 8], [1, 256]]))),
                                writes=[f"uc{ib}"], dma=True)
                        sc.add("sp", (lambda e, i2=i2, ib=ib: e.dma_start(
                            out=rsb[ib][:].rearrange("p (s n) -> p s n", s=2),
                            in_=dap(d_res, (i2 * 256) * D + qq * 512,
                                    [[D, 128], [128 * D, 2], [1, 512]]))),
                            reads=[f"dres{i2}"], writes=[f"rsb{ib}"], dma=True)

                    p23_loads(0)
                    for i2 in range(8):
                        ib = i2 % 2
                        if i2 + 1 < 8:
                            p23_loads(i2 + 1)
                        for sub in range(2):
                            i = i2 * 2 + sub
                            io = i % 2

                            def opmm(e, ib=ib, sub=sub):
                                for k in range(32):
                                    ins = e.matmul(pA[:], yfc[ib][:, k * 256 + sub * 128: k * 256 + sub * 128 + 128],
                                                   wqb[:, k * 512:(k + 1) * 512],
                                                   start=(k == 0), stop=(k == 31))
                                return ins
                            sc.add("pe", opmm, reads=[f"yfc{ib}", "wqb"], writes=["pA"])

                            def fcmm(e, ib=ib, sub=sub):
                                for k in range(64):
                                    ins = e.matmul(pB[:], uc[ib][:, k * 256 + sub * 128: k * 256 + sub * 128 + 128],
                                                   f2b[:, k * 512:(k + 1) * 512],
                                                   start=(k == 0), stop=(k == 63))
                                return ins
                            sc.add("pe", fcmm, reads=[f"uc{ib}", "f2b"], writes=["pB"])
                            oi = ost[io]
                            sc.add("dve", (lambda e, oi=oi, i=i: e.tensor_scalar(
                                oi[:], pA[:], g_cols[:, i:i + 1], None, op0=ALU.mult)),
                                reads=["pA", "g_cols"], writes=[f"ost{io}"])
                            sc.add("dve", (lambda e, oi=oi: e.tensor_add(oi[:], oi[:], pB[:])),
                                   reads=["pB", f"ost{io}"], writes=[f"ost{io}"])
                            sc.add("dve", (lambda e, oi=oi, ib=ib, sub=sub: e.tensor_add(
                                oi[:], oi[:], rsb[ib][:, sub * 512:(sub + 1) * 512])),
                                reads=[f"rsb{ib}", f"ost{io}"], writes=[f"ost{io}"])
                            sc.add("pool", (lambda e, oi=oi, i=i, qq=qq: e.dma_start(
                                out=d_res[i * 128:(i + 1) * 128, qq * 512:(qq + 1) * 512], in_=oi[:])),
                                reads=[f"ost{io}"], writes=[f"dres{i//2}"], dma=True, stream=f"ost{io}")
                    sc.end()

        def emit_gateup(hbuf):
            with ExitStack() as stP:
                sb = lambda n, s, d: stP.enter_context(nc.sbuf_tensor(n, s, d))
                wgb = [sb(f"f_wg{i}", [128, KD * 128], BF16) for i in range(2)]
                wub = [sb(f"f_wu{i}", [128, KD * 128], BF16) for i in range(2)]
                gsb = [sb(f"f_g{i}", [128, 512], BF16) for i in range(2)]
                usb = [sb(f"f_u{i}", [128, 512], BF16) for i in range(2)]
                fsb = [sb(f"f_f{i}", [128, 512], BF16) for i in range(4)]
                sgc = sb("f_sg", [128, 64], F32)
                suc = sb("f_su", [128, 64], F32)
                sc.begin('p5a')
                sc.add("sp", lambda e: e.dma_start(out=sgc[:], in_=sg_c.ap()), writes=["sgc"], dma=True)
                sc.add("sp", lambda e: e.dma_start(out=suc[:], in_=su_c.ap()), writes=["suc"], dma=True)
                for j in range(64):
                    jb = j % 2
                    for (wt, dst, Rw) in ((wg_i, wgb[jb], f"fwg{jb}"), (wu_i, wub[jb], f"fwu{jb}")):
                        for half in range(2):
                            sc.add("pool", (lambda e, wt=wt, dst=dst, j=j, half=half: e.dma_start(
                                out=dst[:, half * 1024:(half + 1) * 1024],
                                in_=dap(wt, j * 128 + half * 8 * 128 * INTER,
                                        [[INTER, 128], [128 * INTER, 8], [1, 128]]))),
                                writes=[Rw], dma=True)
                    for tc in range(4):
                        def gmm(e, jb=jb, tc=tc):
                            for k in range(KD):
                                ins = e.matmul(pA[:], wgb[jb][:, k * 128:(k + 1) * 128],
                                               hbuf[:, k * T + tc * 512:k * T + (tc + 1) * 512],
                                               start=(k == 0), stop=(k == KD - 1))
                            return ins
                        sc.add("pe", gmm, reads=[f"fwg{jb}", "hbuf"], writes=["pA"])

                        def umm(e, jb=jb, tc=tc):
                            for k in range(KD):
                                ins = e.matmul(pB[:], wub[jb][:, k * 128:(k + 1) * 128],
                                               hbuf[:, k * T + tc * 512:k * T + (tc + 1) * 512],
                                               start=(k == 0), stop=(k == KD - 1))
                            return ins
                        sc.add("pe", umm, reads=[f"fwu{jb}", "hbuf"], writes=["pB"])
                        idx = j * 4 + tc
                        gg, uu, ff = gsb[idx % 2], usb[idx % 2], fsb[idx % 4]
                        Rg, Ru, Rf = f"fg{idx%2}", f"fu{idx%2}", f"ff{idx%4}"
                        sc.add("act", (lambda e, gg=gg, j=j: e.activation(gg[:], pA[:], AF.Silu,
                                                                         scale=sgc[:, j:j + 1])),
                               reads=["pA", "sgc"], writes=[Rg])
                        sc.add("act", (lambda e, uu=uu, j=j: e.activation(uu[:], pB[:], AF.Identity,
                                                                         scale=suc[:, j:j + 1])),
                               reads=["pB", "suc"], writes=[Ru])
                        sc.add("dve", (lambda e, ff=ff, gg=gg, uu=uu: e.tensor_mul(ff[:], gg[:], uu[:])),
                               reads=[Rg, Ru], writes=[Rf])
                        sc.add("sp", (lambda e, ff=ff, j=j, tc=tc: e.dma_start(
                            out=d_fuse[j * 128:(j + 1) * 128, tc * 512:(tc + 1) * 512], in_=ff[:])),
                            reads=[Rf], dma=True)
                sc.end()

        def emit_down():
            with ExitStack() as stP:
                sb = lambda n, s, d: stP.enter_context(nc.sbuf_tensor(n, s, d))
                wdb = sb("d_wd", [128, 64 * 512], BF16)
                fcol = [sb(f"d_f{i}", [128, 64 * 256], BF16) for i in range(2)]
                rsb = [sb(f"d_rs{i}", [128, 2 * 512], F32) for i in range(2)]
                ostg = [sb(f"d_o{i}", [128, 512], F32) for i in range(2)]
                sdb = sb("d_sd", [128, D], F32)
                sc.begin('p5b_pre')
                sc.add("sp", lambda e: e.dma_start(out=sdb[:], in_=dap(sd_r, 0, [[0, 128], [1, D]])),
                       writes=["sdb"], dma=True)
                sc.end()
                for qq in range(4):
                    sc.begin(f'p5b_q{qq}')
                    for gset in range(8):
                        sc.add("pool", (lambda e, qq=qq, gset=gset: e.dma_start(
                            out=wdb[:, gset * 4096:(gset + 1) * 4096],
                            in_=dap(wd_i, qq * 512 + gset * 8 * 128 * D,
                                    [[D, 128], [128 * D, 8], [1, 512]]))),
                            writes=["wdb"], dma=True)
                    def p5_loads(i2):
                        ib = i2 % 2
                        for gset in range(8):
                            qeng = "act" if gset % 2 == 0 else "sp"
                            sc.add(qeng, (lambda e, i2=i2, ib=ib, gset=gset: e.dma_start(
                                out=fcol[ib][:, gset * 2048:(gset + 1) * 2048],
                                in_=dap(d_fuse, i2 * 256 + gset * 8 * 128 * T,
                                        [[T, 128], [128 * T, 8], [1, 256]]))),
                                writes=[f"fcol{ib}"], dma=True)
                        sc.add("sp", (lambda e, i2=i2, ib=ib: e.dma_start(
                            out=rsb[ib][:].rearrange("p (s n) -> p s n", s=2),
                            in_=dap(d_res, (i2 * 256) * D + qq * 512,
                                    [[D, 128], [128 * D, 2], [1, 512]]))),
                            writes=[f"drs{ib}"], dma=True)

                    p5_loads(0)
                    for i2 in range(8):
                        ib = i2 % 2
                        if i2 + 1 < 8:
                            p5_loads(i2 + 1)
                        for sub in range(2):
                            i = i2 * 2 + sub
                            io = i % 2

                            def dmm(e, ib=ib, sub=sub):
                                for k in range(64):
                                    ins = e.matmul(pA[:], fcol[ib][:, k * 256 + sub * 128: k * 256 + sub * 128 + 128],
                                                   wdb[:, k * 512:(k + 1) * 512],
                                                   start=(k == 0), stop=(k == 63))
                                return ins
                            sc.add("pe", dmm, reads=[f"fcol{ib}", "wdb"], writes=["pA"])
                            oo = ostg[io]
                            sc.add("dve", (lambda e, oo=oo, qq=qq: e.tensor_mul(
                                oo[:], pA[:], sdb[:, qq * 512:(qq + 1) * 512])),
                                reads=["pA", "sdb"], writes=[f"dost{io}"])
                            sc.add("dve", (lambda e, oo=oo, ib=ib, sub=sub: e.tensor_add(
                                oo[:], oo[:], rsb[ib][:, sub * 512:(sub + 1) * 512])),
                                reads=[f"drs{ib}", f"dost{io}"], writes=[f"dost{io}"])
                            sc.add("pool", (lambda e, oo=oo, i=i, qq=qq: e.dma_start(
                                out=out[i * 128:(i + 1) * 128, qq * 512:(qq + 1) * 512], in_=oo[:])),
                                reads=[f"dost{io}"], dma=True, stream=f"dost{io}")
                    sc.end()

        with ExitStack() as stO:
            sbO = lambda n, s, d: stO.enter_context(nc.sbuf_tensor(n, s, d))
            a_my = sbO("a_my", [NH, T], F32)
            ldt_my = sbO("ldt_my", [NH, T], F32)
            state = sbO("state", [NH, NH * 64], F32)

            with ExitStack() as stH:
                hbuf = stH.enter_context(nc.sbuf_tensor("hbuf1", [128, KD * T], BF16))
                stf = stH.enter_context(nc.sbuf_tensor("stf1", [NH, T], F32))
                if _on("P0"):
                    emit_hpass("p0", hbuf, xb.ap(), 0, affine=True, res_add=True)
                if _on("P1"):
                    emit_inproj("p1", hbuf, list(range(0, 80)) + [OT_DT],
                                [d_zsil, d_xf, d_Bf, d_C], stf)
                    emit_dtprep("dpm", stf, a_my, ldt_my)
                if _on("spill"):
                    sc.begin()
                    sc.add("sp", lambda e: e.dma_start(out=d_h.ap(), in_=hbuf[:]),
                           reads=["hbuf"], dma=True)
                    sc.end()

            with ExitStack() as stPP:
                a_pp = stPP.enter_context(nc.sbuf_tensor("a_pp", [NH, T], F32))
                ldt_pp = stPP.enter_context(nc.sbuf_tensor("ldt_pp", [NH, T], F32))
                with ExitStack() as stH:
                    hbuf = stH.enter_context(nc.sbuf_tensor("hbuf2", [128, KD * T], BF16))
                    stf = stH.enter_context(nc.sbuf_tensor("stf2", [NH, T], F32))
                    if _on("P0p"):
                        emit_hpass("pp", hbuf, xb.ap(), T, affine=True, res_add=False)
                    if _on("P15a"):
                        emit_inproj("pq", hbuf, list(range(OT_X, OT_C)) + [OT_DT],
                                    [None, d_xpf, d_Bpf, None], stf)
                        emit_dtprep("dpp", stf, a_pp, ldt_pp)

                sc.begin()
                sc.add("dve", lambda e: e.memset(state[:], 0.0), writes=["state"])
                sc.end()
                if _on("P15b"):
                    emit_scan("s15", False, a_pp, ldt_pp, d_xpf, d_Bpf, state)
            sc.begin()
            sc.add("dve", lambda e: e.tensor_scalar(state[:], state[:], rm_sb[:], None, op0=ALU.mult),
                   reads=["state", "rm"], writes=["state"])
            sc.end()

            if _on("P3a"):
                with ExitStack() as stH:
                    hbuf = stH.enter_context(nc.sbuf_tensor("hbuf3", [128, KD * T], BF16))
                    sc.begin()
                    sc.add("sp", lambda e: e.dma_start(out=hbuf[:], in_=d_h.ap()),
                           writes=["hbuf"], dma=True)
                    sc.end()
                    emit_mlp(hbuf)

            if _on("P2"):
                emit_scan("s2", True, a_my, ldt_my, d_xf, d_Bf, state)

        if _on("P23b"):
            emit_p23b()

        with ExitStack() as stH:
            hbuf = stH.enter_context(nc.sbuf_tensor("hbuf4", [128, KD * T], BF16))
            if _on("P4"):
                emit_hpass("p4", hbuf, d_res.ap(), 0, affine=False, res_add=False)
            if _on("P5a"):
                emit_gateup(hbuf)
        if _on("P5b"):
            emit_down()

        if taps:
            sc.begin()
            for (th, tout, shp) in taps:
                n = int(np.prod(shp))
                sc.add("sp", (lambda e, th=th, tout=tout, n=n: e.dma_start(
                    out=dap(tout, 0, [[1, n]]), in_=dap(th, 0, [[1, n]]))), dma=True)
            sc.end()

    return nc



def _consts():
    ident = np.eye(128, dtype=np.float32)
    tri = np.zeros((128, 128), np.float32)
    for s_ in range(128):
        tri[s_, s_:] = 1.0
    trineg = np.zeros((128, 128), np.float32)
    for s_ in range(128):
        trineg[s_, :s_] = -1e30
    ones = np.ones((1, 128), np.float32)
    return ident, tri, trineg, ones


def kernel(L0_attn_hidden_states, L0_attn_ln_w, L0_attn_ln_b, L0_attn_in_proj_w,
           L0_attn_dt_bias, L0_attn_A_log, L0_attn_Dp, L0_attn_norm_w,
           L0_attn_out_proj_w, L0_attn_fc1_w, L0_attn_fc1_b, L0_attn_fc2_w,
           L0_attn_fc2_b, L0_ffn_w_gate, L0_ffn_w_up, L0_ffn_w_down,
           L0_ffn_s_gate, L0_ffn_s_up, L0_ffn_s_down):
    global _last_results
    x = np.ascontiguousarray(np.asarray(L0_attn_hidden_states, dtype=np.float32))
    ident, tri, trineg, ones = _consts()
    shared = {
        "winT": np.ascontiguousarray(np.asarray(L0_attn_in_proj_w, np.float32).T),
        "woutT": np.ascontiguousarray(np.asarray(L0_attn_out_proj_w, np.float32).T),
        "fc1T": np.ascontiguousarray(np.asarray(L0_attn_fc1_w, np.float32).T),
        "fc2T": np.ascontiguousarray(np.asarray(L0_attn_fc2_w, np.float32).T),
        "lnw_r": np.ascontiguousarray(np.asarray(L0_attn_ln_w, np.float32)),
        "lnb_r": np.ascontiguousarray(np.asarray(L0_attn_ln_b, np.float32)),
        "dtb_c": np.ascontiguousarray(np.asarray(L0_attn_dt_bias, np.float32).reshape(NH, 1)),
        "alog_c": np.ascontiguousarray(np.asarray(L0_attn_A_log, np.float32).reshape(NH, 1)),
        "dp_r": np.ascontiguousarray(np.asarray(L0_attn_Dp, np.float32).reshape(1, NH)),
        "nw_cols": np.ascontiguousarray(np.asarray(L0_attn_norm_w, np.float32).reshape(32, 128).T),
        "fc1b_c": np.ascontiguousarray(np.asarray(L0_attn_fc1_b, np.float32).reshape(64, 128).T),
        "fc2b_r": np.ascontiguousarray(np.asarray(L0_attn_fc2_b, np.float32)),
        "wg_i": np.ascontiguousarray(np.asarray(L0_ffn_w_gate, np.int8)),
        "wu_i": np.ascontiguousarray(np.asarray(L0_ffn_w_up, np.int8)),
        "wd_i": np.ascontiguousarray(np.asarray(L0_ffn_w_down, np.int8)),
        "sg_c": np.ascontiguousarray(np.asarray(L0_ffn_s_gate, np.float32).reshape(64, 128).T),
        "su_c": np.ascontiguousarray(np.asarray(L0_ffn_s_up, np.float32).reshape(64, 128).T),
        "sd_r": np.ascontiguousarray(np.asarray(L0_ffn_s_down, np.float32)),
        "ident_d": ident, "tri_d": tri, "trineg_d": trineg, "ones_d": ones,
    }
    in_maps = []
    for c in range(8):
        b, H = c // 2, c % 2
        xc = x[b] if H == 0 else np.ascontiguousarray(np.roll(x[b], T, axis=0))
        m = dict(shared)
        m["xb"] = np.ascontiguousarray(xc)
        m["rm_c"] = np.full((NH, 1), 1.0 if H == 1 else 0.0, np.float32)
        in_maps.append(m)
    nc = build_nc()
    cores = list(range(8)) if _DEV_CORES is None else list(_DEV_CORES)
    res = run_bass_kernel_spmd(nc, [in_maps[c] for c in cores], core_ids=list(range(len(cores))))
    _last_results = res
    outp = np.zeros((BATCH, SEQ, D), np.float32)
    for i, c in enumerate(cores):
        b, H = c // 2, c % 2
        outp[b, H * T:(H + 1) * T, :] = res.results[i]["out"]
    return outp
